# version 15
# baseline (speedup 1.0000x reference)
import math
import contextlib

import numpy as np

import concourse.bass as bass
import concourse.mybir as mybir
from concourse.bass_utils import run_bass_kernel_spmd
from concourse.aot_env import allow_ici_all_to_all

F32 = mybir.dt.float32
BF16 = mybir.dt.bfloat16
AF = mybir.ActivationFunctionType
ALU = mybir.AluOpType
AX = mybir.AxisListType

NCORES = 8
DM = 2048
DC = DM // 128
HFF = 1408
HE = HFF // 128
HSH = 2816
HS2 = HSH // 128
NEXP = 8
LN_EPS = 1e-5
UN_EPS = 1e-6
ROPE_BASE = 10000.0
PI = math.pi
TWO_PI = 2.0 * math.pi


class Sem:
    def __init__(self, nc, st, name):
        self.h = st.enter_context(nc.semaphore(name))
        self.v = 0

    def inc(self, n=1):
        self.v += n
        return self.v


class Plan:
    ENGINES = ("tensor", "vector", "scalar", "gpsimd", "sync")

    def __init__(self):
        self.ops = {e: [] for e in self.ENGINES}

    def add(self, eng, fn):
        self.ops[eng].append(fn)

    def wait(self, eng, sem, val):
        if val > 0:
            self.ops[eng].append(lambda e, s=sem.h, v=val: e.wait_ge(s, v))

    def barrier(self, sems):
        for eng in self.ENGINES:
            for s in sems:
                self.wait(eng, s, s.v)


def build_nc(S):
    T = S // NCORES
    NT = S // 512
    NQ = S // 512
    NT8 = T // 128
    tsp = 512 if T >= 1024 else max(64, T // 2)
    tspans = [(i * tsp, min(tsp, T - i * tsp)) for i in range(max(1, T // tsp))]

    nc = bass.Bass("TRN2", target_bir_lowering=False)

    xt = nc.dram_tensor("xt", [NT, 128, DC * 512], F32, kind="ExternalInput")
    xres = nc.dram_tensor("xres", [128, DC * T], F32, kind="ExternalInput")
    wq = nc.dram_tensor("wq", [128, DC * 256], F32, kind="ExternalInput")
    wk = nc.dram_tensor("wk", [128, DC * 256], F32, kind="ExternalInput")
    wv = nc.dram_tensor("wv", [128, DC * 256], F32, kind="ExternalInput")
    wo = nc.dram_tensor("wo", [4, 128, DC * 512], F32, kind="ExternalInput")
    qg = nc.dram_tensor("qg", [1, 2], F32, kind="ExternalInput")
    gw = nc.dram_tensor("gw", [128, DC * NEXP], F32, kind="ExternalInput")
    wgate = nc.dram_tensor("wgate", [NEXP, HE, 128, DC * 128], F32, kind="ExternalInput")
    wup = nc.dram_tensor("wup", [NEXP, HE, 128, DC * 128], F32, kind="ExternalInput")
    wdown = nc.dram_tensor("wdown", [NEXP, HE, 128, DM], F32, kind="ExternalInput")
    swg = nc.dram_tensor("swg", [HS2, 128, DC * 128], F32, kind="ExternalInput")
    swu = nc.dram_tensor("swu", [HS2, 128, DC * 128], F32, kind="ExternalInput")
    swd = nc.dram_tensor("swd", [HS2, 128, DM], F32, kind="ExternalInput")
    out_d = nc.dram_tensor("out", [128, DC * T], F32, kind="ExternalOutput")
    a2a_in = [nc.dram_tensor(f"a2a_in{h}", [NCORES, 128, T], BF16) for h in range(2)]
    a2a_out = [nc.dram_tensor(f"a2a_out{h}", [NCORES, 128, T], BF16) for h in range(2)]

    st = contextlib.ExitStack()
    with st:
        E = st.enter_context

        sq_w = Sem(nc, st, "sq_w")
        sq_x = [Sem(nc, st, f"sq_x{i}") for i in range(2)]
        sq_g = [Sem(nc, st, f"sq_g{i}") for i in range(4)]
        sq_d = Sem(nc, st, "sq_d")
        sy_r = [Sem(nc, st, f"sy_r{i}") for i in range(2)]
        sq_wo = [Sem(nc, st, f"sq_wo{i}") for i in range(2)]
        sy_ld = Sem(nc, st, "sy_ld")
        sy_sh = Sem(nc, st, "sy_sh")
        sy_st = Sem(nc, st, "sy_st")
        s_pe = Sem(nc, st, "s_pe")
        s_act = Sem(nc, st, "s_act")
        s_dve = Sem(nc, st, "s_dve")
        s_gp = Sem(nc, st, "s_gp")
        s_cc = Sem(nc, st, "s_cc")
        ALL_SEMS = ([sq_w] + sq_x + sq_g + [sq_d] + sy_r + sq_wo +
                    [sy_ld, sy_sh, sy_st, s_pe, s_act, s_dve, s_gp, s_cc])

        ones_row = E(nc.sbuf_tensor("ones_row", [1, 128], F32))
        ones_cb = E(nc.sbuf_tensor("ones_cb", [128, 1], BF16))
        ones_cf = E(nc.sbuf_tensor("ones_cf", [128, 1], F32))
        ident_b = E(nc.sbuf_tensor("ident_b", [128, 128], BF16))
        ident_f = E(nc.sbuf_tensor("ident_f", [128, 128], F32))
        perm_a = E(nc.sbuf_tensor("perm_a", [128, 128], F32))
        perm_b = E(nc.sbuf_tensor("perm_b", [128, 128], BF16))
        qg_sb = E(nc.sbuf_tensor("qg_sb", [1, 2], F32))
        gcol = E(nc.sbuf_tensor("gcol", [128, 2], F32))
        gneg = E(nc.sbuf_tensor("gneg", [128, 2], F32))
        masks = E(nc.sbuf_tensor("masks", [128, 4 * 512], BF16))
        fcol = E(nc.sbuf_tensor("fcol", [64, 1], F32))
        f2p = E(nc.sbuf_tensor("f2p", [64, 1], F32))
        jcol = E(nc.sbuf_tensor("jcol", [64, 1], F32))
        gw_sb = E(nc.sbuf_tensor("gw_sb", [128, DC * NEXP], BF16))
        eps6 = E(nc.sbuf_tensor("eps6", [1, 1], F32))
        eps5 = E(nc.sbuf_tensor("eps5", [1, 1], F32))

        plan = Plan()
        P = plan.add
        W = plan.wait

        P("sync", lambda e: e.dma_start(out=qg_sb[:], in_=qg[:]).then_inc(sy_ld.h, 16))
        v_qg = sy_ld.inc(16)
        P("gpsimd", lambda e: e.dma_start(out=gw_sb[:], in_=gw[:]).then_inc(sq_w.h, 16))
        sq_w.inc(16)

        P("vector", lambda e: e.memset(eps6[:], UN_EPS))
        P("vector", lambda e: e.memset(eps5[:], LN_EPS))
        P("vector", lambda e: e.memset(ones_row[:], 1.0))
        P("vector", lambda e: e.memset(ones_cb[:], 1.0))
        P("vector", lambda e: e.memset(ones_cf[:], 1.0).then_inc(s_dve.h, 1))
        v_ones = s_dve.inc()

        for tens in (ident_b, ident_f):
            P("gpsimd", lambda e, t=tens: e.memset(t[:], 1.0).then_inc(s_gp.h, 1))
            v0 = s_gp.inc()
            W("gpsimd", s_gp, v0)
            P("gpsimd", lambda e, t=tens: e.affine_select(
                out=t[:], in_=t[:], pattern=[[1, 128]],
                compare_op=ALU.is_equal, fill=0.0, base=0,
                channel_multiplier=-1).then_inc(s_gp.h, 1))
            s_gp.inc()
        for tens, base in ((perm_a, -64), (perm_b, 64)):
            P("gpsimd", lambda e, t=tens: e.memset(t[:], 1.0).then_inc(s_gp.h, 1))
            v0 = s_gp.inc()
            W("gpsimd", s_gp, v0)
            P("gpsimd", lambda e, t=tens, base=base: e.affine_select(
                out=t[:], in_=t[:], pattern=[[1, 128]],
                compare_op=ALU.is_equal, fill=0.0, base=base,
                channel_multiplier=-1).then_inc(s_gp.h, 1))
            s_gp.inc()
        v_perm_gp = s_gp.v
        W("vector", s_gp, v_perm_gp)
        P("vector", lambda e: e.tensor_tensor(out=perm_a[:], in0=perm_a[:], in1=perm_b[:], op=ALU.add).then_inc(s_dve.h, 1))
        v_perm = s_dve.inc()
        P("gpsimd", lambda e: e.memset(masks[:], 1.0).then_inc(s_gp.h, 1))
        v0 = s_gp.inc()
        W("gpsimd", s_gp, v0)
        for j in range(4):
            P("gpsimd", lambda e, j=j: e.affine_select(
                out=masks[:, j * 512:(j + 1) * 512],
                in_=masks[:, j * 512:(j + 1) * 512], pattern=[[1, 512]],
                compare_op=ALU.is_ge, fill=0.0, base=-128 * j,
                channel_multiplier=-1).then_inc(s_gp.h, 1))
            s_gp.inc()
        v_masks = s_gp.v

        P("gpsimd", lambda e: e.iota(
            jcol[:], [[1, 1]], base=0, channel_multiplier=1,
            allow_small_or_imprecise_dtypes=True).then_inc(s_gp.h, 1))
        v_j = s_gp.inc()
        W("scalar", s_gp, v_j)
        P("scalar", lambda e: e.activation(
            out=fcol[:], in_=jcol[:], func=AF.Exp,
            scale=-math.log(ROPE_BASE) / 64.0, bias=0.0).then_inc(s_act.h, 1))
        v_f0 = s_act.inc()
        W("vector", s_act, v_f0)
        P("vector", lambda e: e.tensor_single_scalar(
            out=f2p[:], in_=fcol[:], scalar=1.0 / TWO_PI, op=ALU.mult).then_inc(s_dve.h, 1))
        v_f = s_dve.inc()

        with contextlib.ExitStack() as st0:
            ps_g = st0.enter_context(nc.psum_tensor("ps_g", [128, 2], F32))
            W("tensor", sy_ld, v_qg)
            W("tensor", s_dve, v_ones)
            P("tensor", lambda e: e.matmul(ps_g[:], ones_row[:], qg_sb[:],
                                           start=True, stop=True).then_inc(s_pe.h, 1))
            v0 = s_pe.inc()
            W("scalar", s_pe, v0)
            P("scalar", lambda e: e.copy(out=gcol[:], in_=ps_g[:]).then_inc(s_act.h, 1))
            s_act.inc()
            P("scalar", lambda e: e.activation(out=gneg[:], in_=ps_g[:], func=AF.Abs,
                                               scale=1.0, bias=0.0).then_inc(s_act.h, 1))
            v0 = s_act.inc()
            W("vector", s_act, v0)
            P("vector", lambda e: e.tensor_single_scalar(
                out=gneg[:], in_=gneg[:], scalar=-1.0, op=ALU.mult).then_inc(s_dve.h, 1))
            s_dve.inc()
            plan.barrier([s_pe, s_act, s_dve])

            with contextlib.ExitStack() as stBC:
                EBC = stBC.enter_context
                QT = EBC(nc.sbuf_tensor("QT", [128, 2 * S], BF16))
                KT = EBC(nc.sbuf_tensor("KT", [128, 2 * S], BF16))
                VT = EBC(nc.sbuf_tensor("VT", [128, 2 * S], BF16))

                with contextlib.ExitStack() as stB:
                    EB = stB.enter_context
                    wq_sb = EB(nc.sbuf_tensor("wq_sb", [128, DC * 256], BF16))
                    wk_sb = EB(nc.sbuf_tensor("wk_sb", [128, DC * 256], BF16))
                    wv_sb = EB(nc.sbuf_tensor("wv_sb", [128, DC * 256], BF16))
                    xbuf = [EB(nc.sbuf_tensor(f"xbuf{i}", [128, DC * 512], BF16)) for i in range(2)]
                    ps_qkv = [EB(nc.psum_tensor(f"ps_qkv{i}", [128, 512], F32)) for i in range(4)]
                    ps_ss = EB(nc.psum_tensor("ps_ss", [1, 512], F32))
                    ps_bc = [EB(nc.psum_tensor("ps_bc0", [128, 512], F32))] * 2
                    ps_vt = EB(nc.psum_tensor("ps_vt", [128, 512], BF16))
                    sq_b = [EB(nc.sbuf_tensor(f"sq_b{i}", [128, 512], BF16)) for i in range(2)]
                    rn_r = EB(nc.sbuf_tensor("rn_r", [1, 512], F32))
                    rn2_r = EB(nc.sbuf_tensor("rn2_r", [1, 512], F32))
                    bc_sb = [EB(nc.sbuf_tensor(f"bc_sb{i}", [128, 512], F32)) for i in range(2)]
                    qn_f = EB(nc.sbuf_tensor("qn_f", [128, 512], F32))
                    t64a = EB(nc.sbuf_tensor("t64a", [64, 512], F32))
                    t64b = EB(nc.sbuf_tensor("t64b", [64, 512], F32))
                    t64c = EB(nc.sbuf_tensor("t64c", [64, 512], F32))
                    t64d = EB(nc.sbuf_tensor("t64d", [64, 512], F32))
                    v_sb = [EB(nc.sbuf_tensor(f"v_sb{i}", [128, 512], BF16)) for i in range(2)]
                    pos_t = EB(nc.sbuf_tensor("pos_t", [64, 512], F32))
                    th_t = EB(nc.sbuf_tensor("th_t", [64, 512], F32))
                    th2_t = EB(nc.sbuf_tensor("th2_t", [64, 512], F32))
                    m_t = EB(nc.sbuf_tensor("m_t", [64, 512], F32))
                    m2_t = EB(nc.sbuf_tensor("m2_t", [64, 512], F32))
                    k_t = EB(nc.sbuf_tensor("k_t", [64, 512], F32))
                    k2_t = EB(nc.sbuf_tensor("k2_t", [64, 512], F32))
                    cosT = [EB(nc.sbuf_tensor(f"cosT{i}", [64, 512], F32)) for i in range(2)]
                    sinT = [EB(nc.sbuf_tensor(f"sinT{i}", [64, 512], F32)) for i in range(2)]
                    i32a = EB(nc.sbuf_tensor("i32a", [64, 512], mybir.dt.int32))
                    i32b = EB(nc.sbuf_tensor("i32b", [64, 512], mybir.dt.int32))

                    for w_d, w_s in ((wq, wq_sb), (wk, wk_sb), (wv, wv_sb)):
                        P("gpsimd", lambda e, a=w_d, b=w_s: e.dma_start(
                            out=b[:], in_=a[:]).then_inc(sq_w.h, 16))
                        sq_w.inc(16)
                    v_wqkv = sq_w.v

                    xload = [0, 0]
                    pe_at_tt_end = []
                    dve_at_tt_end = []
                    grp_cons = {}
                    bc_cons = {}
                    bc_bank_cons = {}
                    vtm_done = [None]
                    qk_count = [0]
                    ss_done = {}
                    sqrt_done = {}
                    recip_done = {}
                    bcmm_done = {}
                    vcopy_slot_cons = {}
                    tr_done = {}
                    swap_done = [0]

                    def load_xtile(tt):
                        slot = tt % 2
                        if tt >= 2:
                            W("gpsimd", s_pe, pe_at_tt_end[tt - 2])
                        P("gpsimd", lambda e, tt=tt, slot=slot: e.dma_start(
                            out=xbuf[slot][:], in_=xt[tt]).then_inc(sq_x[slot].h, 16))
                        xload[slot] += 1
                        sq_x[slot].inc(16)

                    load_xtile(0)
                    if NT > 1:
                        load_xtile(1)

                    gidx = 0
                    v_tab_prev = 0
                    v_L1_prev = 0
                    post0 = {}
                    post1 = {}
                    post2 = {}
                    for tt in range(NT):
                        slot = tt % 2
                        tslot = tt % 2
                        if tt >= 1:
                            W("gpsimd", s_dve, v_L1_prev)
                        P("gpsimd", lambda e, tt=tt: e.iota(
                            pos_t[:], [[1, 512]], base=tt * 512, channel_multiplier=0,
                            allow_small_or_imprecise_dtypes=True).then_inc(s_gp.h, 1))
                        v_pos = s_gp.inc()
                        W("vector", s_gp, v_pos)
                        W("vector", s_dve, v_f)
                        if tt >= 1:
                            W("vector", s_act, v_tab_prev)
                        P("vector", lambda e: e.tensor_single_scalar(out=th_t[:], in_=pos_t[:], scalar=fcol[:, 0:1], op=ALU.mult))
                        P("vector", lambda e: e.tensor_scalar(
                            out=th2_t[:], in0=pos_t[:], scalar1=fcol[:, 0:1],
                            scalar2=PI / 2.0, op0=ALU.mult, op1=ALU.add))
                        P("vector", lambda e: e.tensor_single_scalar(out=m_t[:], in_=pos_t[:], scalar=f2p[:, 0:1], op=ALU.mult))
                        P("vector", lambda e: e.tensor_scalar(
                            out=m2_t[:], in0=pos_t[:], scalar1=f2p[:, 0:1],
                            scalar2=0.25, op0=ALU.mult, op1=ALU.add).then_inc(s_dve.h, 1))
                        v_L1_prev = s_dve.inc()
                        W("vector", s_dve, v_L1_prev)
                        P("vector", lambda e: e.tensor_copy(out=i32a[:], in_=m_t[:]))
                        P("vector", lambda e: e.tensor_copy(out=i32b[:], in_=m2_t[:]).then_inc(s_dve.h, 1))
                        W("vector", s_dve, s_dve.inc())
                        P("vector", lambda e: e.tensor_copy(out=k_t[:], in_=i32a[:]))
                        P("vector", lambda e: e.tensor_copy(out=k2_t[:], in_=i32b[:]).then_inc(s_dve.h, 1))
                        W("vector", s_dve, s_dve.inc())
                        P("vector", lambda e: e.scalar_tensor_tensor(
                            out=m_t[:], in0=k_t[:], scalar=-TWO_PI, in1=th_t[:],
                            op0=ALU.mult, op1=ALU.add))
                        P("vector", lambda e: e.scalar_tensor_tensor(
                            out=m2_t[:], in0=k2_t[:], scalar=-TWO_PI, in1=th2_t[:],
                            op0=ALU.mult, op1=ALU.add).then_inc(s_dve.h, 1))
                        W("vector", s_dve, s_dve.inc())
                        P("vector", lambda e: e.tensor_single_scalar(out=k_t[:], in_=m_t[:], scalar=PI, op=ALU.is_ge))
                        P("vector", lambda e: e.tensor_single_scalar(out=k2_t[:], in_=m2_t[:], scalar=PI, op=ALU.is_ge).then_inc(s_dve.h, 1))
                        W("vector", s_dve, s_dve.inc())
                        P("vector", lambda e: e.scalar_tensor_tensor(
                            out=th_t[:], in0=k_t[:], scalar=-TWO_PI, in1=m_t[:],
                            op0=ALU.mult, op1=ALU.add))
                        P("vector", lambda e: e.scalar_tensor_tensor(
                            out=th2_t[:], in0=k2_t[:], scalar=-TWO_PI, in1=m2_t[:],
                            op0=ALU.mult, op1=ALU.add).then_inc(s_dve.h, 1))
                        v_rs = s_dve.inc()
                        W("scalar", s_dve, v_rs)
                        P("scalar", lambda e, tslot=tslot: e.activation(out=sinT[tslot][:], in_=th_t[:], func=AF.Sin,
                                                           scale=1.0, bias=0.0))
                        P("scalar", lambda e, tslot=tslot: e.activation(out=cosT[tslot][:], in_=th2_t[:], func=AF.Sin,
                                                           scale=1.0, bias=0.0).then_inc(s_act.h, 1))
                        v_tab = s_act.inc()
                        v_tab_prev = v_tab

                        groups = [(wq_sb, 0, 0), (wq_sb, 1, 0),
                                  (wk_sb, 0, 1), (wk_sb, 1, 1),
                                  (wv_sb, 0, 2), (wv_sb, 1, 2)]
                        for (wsb, hs, kind) in groups:
                            g = gidx
                            if (g - 3) in post2:
                                post2.pop(g - 3)()
                            if (g - 2) in post1:
                                post1.pop(g - 2)()
                            if (g - 1) in post0:
                                post0.pop(g - 1)()
                            pslot = g % 4
                            qk_i = qk_count[0] if kind < 2 else None
                            W("tensor", sq_x[slot], 16 * xload[slot])
                            if g == 0:
                                W("tensor", sq_w, v_wqkv)
                                W("tensor", s_gp, v_masks)
                            if (g - 4) in grp_cons:
                                cs, cv = grp_cons.pop(g - 4)
                                W("tensor", cs, cv)
                            for dc in range(DC):
                                if dc < DC - 1:
                                    P("tensor", lambda e, wsb=wsb, hs=hs, dc=dc, pslot=pslot, slot=slot: e.matmul(
                                        ps_qkv[pslot][:],
                                        wsb[:, dc * 256 + hs * 128: dc * 256 + hs * 128 + 128],
                                        xbuf[slot][:, dc * 512: dc * 512 + 512],
                                        start=(dc == 0), stop=False))
                                else:
                                    P("tensor", lambda e, wsb=wsb, hs=hs, dc=dc, pslot=pslot, slot=slot: e.matmul(
                                        ps_qkv[pslot][:],
                                        wsb[:, dc * 256 + hs * 128: dc * 256 + hs * 128 + 128],
                                        xbuf[slot][:, dc * 512: dc * 512 + 512],
                                        start=False, stop=True).then_inc(s_pe.h, 1))
                            v_mm = s_pe.inc()

                            if kind < 2:
                                qk_count[0] += 1
                                sqs = qk_i % 2
                                v_sq_box = [None]

                                def mk_post0_qk(g=g, qk_i=qk_i, sqs=sqs, pslot=pslot, v_mm=v_mm, v_sq_box=v_sq_box):
                                    def f():
                                        W("scalar", s_pe, v_mm)
                                        if (qk_i - 2) in ss_done:
                                            W("scalar", s_pe, ss_done[qk_i - 2])
                                        P("scalar", lambda e, pslot=pslot, sqs=sqs: e.activation(
                                            out=sq_b[sqs][:], in_=ps_qkv[pslot][:], func=AF.Square,
                                            scale=1.0, bias=0.0).then_inc(s_act.h, 1))
                                        v_sq_box[0] = s_act.inc()
                                    return f
                                post0[g] = mk_post0_qk()

                                def mk_post1_qk(g=g, qk_i=qk_i, sqs=sqs, v_sq_box=v_sq_box):
                                    def f():
                                        W("tensor", s_act, v_sq_box[0])
                                        if (qk_i - 1) in sqrt_done:
                                            W("tensor", s_act, sqrt_done[qk_i - 1])
                                        P("tensor", lambda e, sqs=sqs: e.matmul(
                                            ps_ss[:], ones_cb[:], sq_b[sqs][:],
                                            start=True, stop=True).then_inc(s_pe.h, 1))
                                        v_ssv = s_pe.inc()
                                        ss_done[qk_i] = v_ssv
                                        W("scalar", s_pe, v_ssv)
                                        if (qk_i - 1) in recip_done:
                                            W("scalar", s_dve, recip_done[qk_i - 1])
                                        P("scalar", lambda e: e.activation(
                                            out=rn_r[:], in_=ps_ss[:], func=AF.Sqrt,
                                            scale=1.0, bias=eps6[:, 0:1]).then_inc(s_act.h, 1))
                                        v_rn0 = s_act.inc()
                                        sqrt_done[qk_i] = v_rn0
                                        W("vector", s_act, v_rn0)
                                        if (qk_i - 1) in bcmm_done:
                                            W("vector", s_pe, bcmm_done[qk_i - 1])
                                        P("vector", lambda e: e.reciprocal(rn2_r[:], rn_r[:]).then_inc(s_dve.h, 1))
                                        recip_done[qk_i] = s_dve.inc()
                                    return f
                                post1[g] = mk_post1_qk()

                                def mk_post2_qk(g=g, qk_i=qk_i, pslot=pslot, hs=hs,
                                                kind=kind, tt=tt, tslot=tslot, v_tab=v_tab):
                                    def f():
                                        bslot = qk_i % 2
                                        if bslot in bc_bank_cons:
                                            cs, cv = bc_bank_cons[bslot]
                                            W("tensor", cs, cv)
                                        W("tensor", s_dve, recip_done[qk_i])
                                        P("tensor", lambda e, bslot=bslot: e.matmul(
                                            ps_bc[bslot][:], ones_row[:], rn2_r[:],
                                            start=True, stop=True).then_inc(s_pe.h, 1))
                                        v_bc = s_pe.inc()
                                        bcmm_done[qk_i] = v_bc
                                        W("scalar", s_pe, v_bc)
                                        if bslot in bc_cons:
                                            cs, cv = bc_cons[bslot]
                                            W("scalar", cs, cv)
                                        P("scalar", lambda e, bslot=bslot: e.copy(
                                            out=bc_sb[bslot][:], in_=ps_bc[bslot][:]).then_inc(s_act.h, 1))
                                        v_bcp = s_act.inc()
                                        bc_bank_cons[bslot] = (s_act, v_bcp)
                                        W("vector", s_act, v_bcp)
                                        dst = QT if kind == 0 else KT
                                        if hs == 1:
                                            P("vector", lambda e, pslot=pslot, bslot=bslot, dst=dst, tt=tt: e.tensor_tensor(
                                                out=dst[:, S + tt * 512: S + tt * 512 + 512],
                                                in0=ps_qkv[pslot][:], in1=bc_sb[bslot][:],
                                                op=ALU.mult).then_inc(s_dve.h, 1))
                                            v_mul = s_dve.inc()
                                            grp_cons[g] = (s_dve, v_mul)
                                            bc_cons[bslot] = (s_dve, v_mul)
                                        else:
                                            W("vector", s_pe, swap_done[0])
                                            P("vector", lambda e, pslot=pslot, bslot=bslot: e.tensor_tensor(
                                                out=qn_f[:], in0=ps_qkv[pslot][:], in1=bc_sb[bslot][:],
                                                op=ALU.mult).then_inc(s_dve.h, 1))
                                            v_mul = s_dve.inc()
                                            grp_cons[g] = (s_dve, v_mul)
                                            W("tensor", s_dve, v_mul)
                                            P("tensor", lambda e, bslot=bslot: e.matmul(
                                                ps_bc[bslot][:], perm_a[:], qn_f[:],
                                                start=True, stop=True).then_inc(s_pe.h, 1))
                                            v_swap = s_pe.inc()
                                            swap_done[0] = v_swap
                                            W("vector", s_pe, v_swap)
                                            W("vector", s_act, v_tab)
                                            P("vector", lambda e, tslot=tslot: e.tensor_tensor(out=t64a[:], in0=qn_f[0:64, :], in1=cosT[tslot][:], op=ALU.mult))
                                            P("vector", lambda e, tslot=tslot, bslot=bslot: e.tensor_tensor(out=t64b[:], in0=ps_bc[bslot][0:64, :], in1=sinT[tslot][:], op=ALU.mult))
                                            P("vector", lambda e, tslot=tslot: e.tensor_tensor(out=t64c[:], in0=qn_f[0:64, :], in1=sinT[tslot][:], op=ALU.mult))
                                            P("vector", lambda e, tslot=tslot, bslot=bslot: e.tensor_tensor(out=t64d[:], in0=ps_bc[bslot][0:64, :], in1=cosT[tslot][:], op=ALU.mult).then_inc(s_dve.h, 1))
                                            W("vector", s_dve, s_dve.inc())
                                            P("vector", lambda e, dst=dst, tt=tt: e.tensor_tensor(
                                                out=dst[0:64, tt * 512: tt * 512 + 512],
                                                in0=t64a[:], in1=t64b[:], op=ALU.subtract))
                                            P("vector", lambda e, dst=dst, tt=tt: e.tensor_tensor(
                                                out=dst[64:128, tt * 512: tt * 512 + 512],
                                                in0=t64c[:], in1=t64d[:], op=ALU.add).then_inc(s_dve.h, 1))
                                            v_rope = s_dve.inc()
                                            bc_cons[bslot] = (s_dve, v_rope)
                                            bc_bank_cons[bslot] = (s_dve, v_rope)
                                    return f
                                post2[g] = mk_post2_qk()
                            else:
                                vs = g % 2
                                v_cp_box = [None]

                                def mk_post0_v(g=g, vs=vs, pslot=pslot, v_mm=v_mm, v_cp_box=v_cp_box):
                                    def f():
                                        W("scalar", s_pe, v_mm)
                                        if vs in vcopy_slot_cons:
                                            W("scalar", s_pe, vcopy_slot_cons[vs])
                                        P("scalar", lambda e, pslot=pslot, vs=vs: e.copy(
                                            out=v_sb[vs][:], in_=ps_qkv[pslot][:]).then_inc(s_act.h, 1))
                                        v_cp_box[0] = s_act.inc()
                                        grp_cons[g] = (s_act, v_cp_box[0])
                                    return f
                                post0[g] = mk_post0_v()

                                def mk_post1_v(g=g, vs=vs, v_cp_box=v_cp_box, hs=hs, tt=tt):
                                    def f():
                                        W("tensor", s_act, v_cp_box[0])
                                        if vtm_done[0] is not None:
                                            W("tensor", vtm_done[0][0], vtm_done[0][1])
                                        for b in range(4):
                                            if b < 3:
                                                P("tensor", lambda e, b=b, vs=vs: e.transpose(
                                                    ps_vt[:, b * 128:(b + 1) * 128],
                                                    v_sb[vs][:, b * 128:(b + 1) * 128], ident_b[:]))
                                            else:
                                                P("tensor", lambda e, b=b, vs=vs: e.transpose(
                                                    ps_vt[:, b * 128:(b + 1) * 128],
                                                    v_sb[vs][:, b * 128:(b + 1) * 128], ident_b[:]).then_inc(s_pe.h, 1))
                                        tr_done[g] = s_pe.inc()
                                        vcopy_slot_cons[vs] = tr_done[g]
                                    return f
                                post1[g] = mk_post1_v()

                                def mk_post2_v(g=g, hs=hs, tt=tt):
                                    def f():
                                        W("vector", s_pe, tr_done[g])
                                        P("vector", lambda e, hs=hs, tt=tt: e.tensor_copy(
                                            out=VT[:, hs * S + tt * 512: hs * S + tt * 512 + 512],
                                            in_=ps_vt[:]).then_inc(s_dve.h, 1))
                                        vtm_done[0] = (s_dve, s_dve.inc())
                                    return f
                                post2[g] = mk_post2_v()

                            gidx += 1

                        if tt == NT - 1:
                            G = gidx
                            post0.pop(G - 1)()
                            post2.pop(G - 3)()
                            post1.pop(G - 2)()
                            post2.pop(G - 2)()
                            post1.pop(G - 1)()
                            post2.pop(G - 1)()

                        pe_at_tt_end.append(s_pe.v)
                        dve_at_tt_end.append(s_dve.v)
                        if tt + 2 < NT:
                            load_xtile(tt + 2)

                plan.barrier(ALL_SEMS)

                with contextlib.ExitStack() as stC:
                    EC = stC.enter_context
                    ps_L = [EC(nc.psum_tensor(f"ps_L{i}", [128, 512], F32)) for i in range(2)]
                    ps_y = [EC(nc.psum_tensor(f"ps_y{i}", [128, 512], F32)) for i in range(2)]
                    ps_den = [EC(nc.psum_tensor(f"ps_den{i}", [1, 512], F32)) for i in range(2)]
                    ps_nb = EC(nc.psum_tensor("ps_nb", [128, 512], F32))
                    E_sb = [EC(nc.sbuf_tensor(f"E_sb{i}", [128, 512], BF16)) for i in range(2)]
                    rd_r = [EC(nc.sbuf_tensor(f"rd_r{q}", [1, 512], F32)) for q in range(2)]
                    nb_sb = [EC(nc.sbuf_tensor(f"nb_sb{q}", [128, 512], F32)) for q in range(2)]
                    yT_sb = [EC(nc.sbuf_tensor(f"yT_sb{q}", [128, 512], BF16)) for q in range(2)]

                    v_y_slot = [0, 0]
                    v_nbcopy = [0, 0]
                    v_rd_slot = [0, 0]
                    pending_tail = [None]

                    for hs in range(2):
                        for qc in range(NQ):
                            K_kt = 4 * qc + 4
                            sl = qc % 2
                            exp_done = {}
                            rdy = {}
                            cons = {}
                            log_done = {}

                            def mk_tail(hs=hs, qc=qc, sl=sl, cons=cons, K_kt=K_kt):
                                def f():
                                    W("vector", s_pe, cons[K_kt - 1])
                                    P("vector", lambda e, sl=sl: e.reciprocal(
                                        rd_r[sl][:], ps_den[sl][:]).then_inc(s_dve.h, 1))
                                    v_rd = s_dve.inc()
                                    v_rd_slot[sl] = v_rd
                                    W("tensor", s_dve, v_rd)
                                    W("tensor", s_act, v_nbcopy[1 - sl])
                                    P("tensor", lambda e, sl=sl: e.matmul(
                                        ps_nb[:], ones_row[:], rd_r[sl][:],
                                        start=True, stop=True).then_inc(s_pe.h, 1))
                                    v_nbm = s_pe.inc()
                                    W("scalar", s_pe, v_nbm)
                                    W("scalar", s_dve, v_y_slot[sl])
                                    P("scalar", lambda e, sl=sl: e.copy(
                                        out=nb_sb[sl][:], in_=ps_nb[:]).then_inc(s_act.h, 1))
                                    v_nbcopy[sl] = s_act.inc()
                                    W("vector", s_act, v_nbcopy[sl])
                                    W("vector", sy_st, sy_st.v)
                                    P("vector", lambda e, sl=sl: e.tensor_tensor(
                                        out=yT_sb[sl][:], in0=ps_y[sl][:], in1=nb_sb[sl][:],
                                        op=ALU.mult).then_inc(s_dve.h, 1))
                                    v_y = s_dve.inc()
                                    v_y_slot[sl] = v_y
                                    W("sync", s_dve, v_y)
                                    q0 = qc * 512
                                    off = 0
                                    while off < 512:
                                        cp = (q0 + off) // T
                                        seg = min(512 - off, T - ((q0 + off) % T))
                                        to = (q0 + off) % T
                                        P("sync", lambda e, cp=cp, hs=hs, to=to, off=off, seg=seg, sl=sl: e.dma_start(
                                            out=a2a_in[hs][cp, :, to:to + seg],
                                            in_=yT_sb[sl][:, off:off + seg]).then_inc(sy_st.h, 16))
                                        sy_st.inc(16)
                                        off += seg
                                return f

                            def emit_cons(kt, start, stop, hs=hs, sl=sl, cons=cons, rdy=rdy):
                                cs, cv = rdy[kt]
                                W("tensor", cs, cv)
                                if start:
                                    W("tensor", s_dve, v_y_slot[sl])
                                    W("tensor", s_dve, v_rd_slot[sl])
                                P("tensor", lambda e, kt=kt, sl=sl, start=start, stop=stop: e.matmul(
                                    ps_den[sl][:], ones_cb[:], E_sb[kt % 2][:],
                                    start=start, stop=stop))
                                P("tensor", lambda e, kt=kt, hs=hs, sl=sl, start=start, stop=stop: e.matmul(
                                    ps_y[sl][:], VT[:, hs * S + kt * 128: hs * S + kt * 128 + 128],
                                    E_sb[kt % 2][:], start=start, stop=stop).then_inc(s_pe.h, 1))
                                cons[kt] = s_pe.inc()

                            for kt in range(K_kt):
                                if kt >= 2:
                                    W("tensor", s_act, exp_done[kt - 2])
                                P("tensor", lambda e, kt=kt, hs=hs, qc=qc: e.matmul(
                                    ps_L[kt % 2][:],
                                    KT[:, hs * S + kt * 128: hs * S + kt * 128 + 128],
                                    QT[:, hs * S + qc * 512: hs * S + qc * 512 + 512],
                                    start=True, stop=True).then_inc(s_pe.h, 1))
                                log_done[kt] = s_pe.inc()
                                if kt >= 1:
                                    emit_cons(kt - 1, start=(kt == 1), stop=False)
                                if kt == 2 and pending_tail[0] is not None:
                                    pending_tail[0]()
                                    pending_tail[0] = None
                                W("scalar", s_pe, log_done[kt])
                                if kt >= 2:
                                    W("scalar", s_pe, cons.get(kt - 2, 0))
                                P("scalar", lambda e, kt=kt, hs=hs: e.activation(
                                    out=E_sb[kt % 2][:], in_=ps_L[kt % 2][:], func=AF.Exp,
                                    scale=gcol[:, hs:hs + 1], bias=gneg[:, hs:hs + 1]).then_inc(s_act.h, 1))
                                exp_done[kt] = s_act.inc()
                                j = kt - 4 * qc
                                if j >= 0:
                                    W("vector", s_act, exp_done[kt])
                                    P("vector", lambda e, kt=kt, j=j: e.tensor_tensor(
                                        out=E_sb[kt % 2][:], in0=E_sb[kt % 2][:],
                                        in1=masks[:, j * 512:(j + 1) * 512],
                                        op=ALU.mult).then_inc(s_dve.h, 1))
                                    rdy[kt] = (s_dve, s_dve.inc())
                                else:
                                    rdy[kt] = (s_act, exp_done[kt])
                            emit_cons(K_kt - 1, start=False, stop=True)
                            pending_tail[0] = mk_tail()

                        pending_tail[0]()
                        pending_tail[0] = None
                        W("gpsimd", sy_st, sy_st.v)

                        def _do_a2a(e, hs=hs):
                            with allow_ici_all_to_all():
                                e.collective_compute(
                                    "AllToAll", ALU.bypass,
                                    replica_groups=[list(range(NCORES))],
                                    ins=[a2a_in[hs].ap().opt()],
                                    outs=[a2a_out[hs].ap().opt()]).then_inc(s_cc.h, 1)
                        P("gpsimd", _do_a2a)
                        s_cc.inc()

            x1 = E(nc.sbuf_tensor("x1", [128, DC * T], F32))
            x1_bf = E(nc.sbuf_tensor("x1_bf", [128, DC * T], BF16))
            norm_done = [0]

            def emit_layernorm(xbuf_t, out_bf, tag, store_cb=None):
                with contextlib.ExitStack() as stX:
                    EX = stX.enter_context
                    ps_sum = [EX(nc.psum_tensor(f"lps_s{i}{tag}", [1, 512], F32)) for i in range(len(tspans))]
                    ps_sq = [EX(nc.psum_tensor(f"lps_q{i}{tag}", [1, 512], F32)) for i in range(len(tspans))]
                    ps_bm = EX(nc.psum_tensor(f"lps_bm{tag}", [128, 512], F32))
                    ps_br = EX(nc.psum_tensor(f"lps_br{tag}", [128, 512], F32))
                    sqb = [EX(nc.sbuf_tensor(f"ln_sqb{i}{tag}", [128, 512], BF16)) for i in range(2)]
                    mean_r = EX(nc.sbuf_tensor(f"ln_mean{tag}", [1, T], F32))
                    ex2_r = EX(nc.sbuf_tensor(f"ln_ex2{tag}", [1, T], F32))
                    rstd_r = EX(nc.sbuf_tensor(f"ln_rstd{tag}", [1, T], F32))
                    msq_r = EX(nc.sbuf_tensor(f"ln_msq{tag}", [1, T], F32))
                    tmp2_f = [EX(nc.sbuf_tensor(f"ln_tmp{i}{tag}", [128, 512], F32)) for i in range(2)]

                    v_acc = {}
                    v_sqmm = {}
                    for ti, (t0, tl) in enumerate(tspans):
                        for dc in range(DC):
                            gi = ti * DC + dc
                            if gi >= 2:
                                W("scalar", s_pe, v_sqmm[gi - 2])
                            P("scalar", lambda e, dc=dc, t0=t0, tl=tl, gi=gi: e.activation(
                                out=sqb[gi % 2][:, 0:tl], in_=xbuf_t[:, dc * T + t0: dc * T + t0 + tl],
                                func=AF.Square, scale=1.0, bias=0.0).then_inc(s_act.h, 1))
                            v_sq = s_act.inc()
                            P("tensor", lambda e, dc=dc, ti=ti, t0=t0, tl=tl: e.matmul(
                                ps_sum[ti][:, 0:tl], ones_cf[:],
                                xbuf_t[:, dc * T + t0: dc * T + t0 + tl],
                                start=(dc == 0), stop=(dc == DC - 1)))
                            W("tensor", s_act, v_sq)
                            P("tensor", lambda e, dc=dc, ti=ti, tl=tl, gi=gi: e.matmul(
                                ps_sq[ti][:, 0:tl], ones_cb[:], sqb[gi % 2][:, 0:tl],
                                start=(dc == 0), stop=(dc == DC - 1)).then_inc(s_pe.h, 1))
                            v_sqmm[gi] = s_pe.inc()
                        v_acc[ti] = v_sqmm[ti * DC + DC - 1]

                    v_bc_prev = 0
                    for ti, (t0, tl) in enumerate(tspans):
                        W("scalar", s_pe, v_acc[ti])
                        P("scalar", lambda e, ti=ti, t0=t0, tl=tl: e.activation(
                            out=mean_r[:, t0:t0 + tl], in_=ps_sum[ti][:, 0:tl],
                            func=AF.Identity, scale=1.0 / DM, bias=0.0))
                        P("scalar", lambda e, ti=ti, t0=t0, tl=tl: e.activation(
                            out=ex2_r[:, t0:t0 + tl], in_=ps_sq[ti][:, 0:tl],
                            func=AF.Identity, scale=1.0 / DM, bias=0.0).then_inc(s_act.h, 1))
                        v_st = s_act.inc()
                        W("vector", s_act, v_st)
                        P("vector", lambda e, t0=t0, tl=tl: e.tensor_tensor(
                            out=msq_r[:, t0:t0 + tl], in0=mean_r[:, t0:t0 + tl],
                            in1=mean_r[:, t0:t0 + tl], op=ALU.mult).then_inc(s_dve.h, 1))
                        v_msq = s_dve.inc()
                        W("vector", s_dve, v_msq)
                        P("vector", lambda e, t0=t0, tl=tl: e.tensor_tensor(
                            out=ex2_r[:, t0:t0 + tl], in0=ex2_r[:, t0:t0 + tl],
                            in1=msq_r[:, t0:t0 + tl], op=ALU.subtract).then_inc(s_dve.h, 1))
                        v_var = s_dve.inc()
                        W("scalar", s_dve, v_var)
                        P("scalar", lambda e, t0=t0, tl=tl: e.activation(
                            out=msq_r[:, t0:t0 + tl], in_=ex2_r[:, t0:t0 + tl],
                            func=AF.Sqrt, scale=1.0, bias=eps5[:, 0:1]).then_inc(s_act.h, 1))
                        v_sd = s_act.inc()
                        W("vector", s_act, v_sd)
                        P("vector", lambda e, t0=t0, tl=tl: e.reciprocal(
                            rstd_r[:, t0:t0 + tl], msq_r[:, t0:t0 + tl]).then_inc(s_dve.h, 1))
                        v_rstd = s_dve.inc()
                        W("tensor", s_dve, v_rstd)
                        if ti >= 1:
                            W("tensor", s_dve, norm_done[0])
                        P("tensor", lambda e, t0=t0, tl=tl: e.matmul(
                            ps_bm[:, 0:tl], ones_row[:], mean_r[:, t0:t0 + tl],
                            start=True, stop=True))
                        P("tensor", lambda e, t0=t0, tl=tl: e.matmul(
                            ps_br[:, 0:tl], ones_row[:], rstd_r[:, t0:t0 + tl],
                            start=True, stop=True).then_inc(s_pe.h, 1))
                        v_bc = s_pe.inc()
                        W("vector", s_pe, v_bc)
                        v_sub_at = {}
                        v_mult_at = {}

                        def _norm_mult(dc, t0=t0, tl=tl, ti=ti):
                            W("vector", s_dve, v_sub_at[dc])
                            P("vector", lambda e, dc=dc: e.tensor_tensor(
                                out=xbuf_t[:, dc * T + t0: dc * T + t0 + tl],
                                in0=tmp2_f[dc % 2][:, 0:tl], in1=ps_br[:, 0:tl],
                                op=ALU.mult).then_inc(s_dve.h, 1))
                            v_n = s_dve.inc()
                            v_mult_at[dc] = v_n
                            if store_cb is not None and ti == len(tspans) - 1:
                                store_cb(dc, v_n)
                            if out_bf is not None:
                                W("scalar", s_dve, v_n)
                                P("scalar", lambda e, dc=dc: e.copy(
                                    out=out_bf[:, dc * T + t0: dc * T + t0 + tl],
                                    in_=xbuf_t[:, dc * T + t0: dc * T + t0 + tl]).then_inc(s_act.h, 1))
                                s_act.inc()

                        for dc in range(DC):
                            if dc >= 2:
                                W("vector", s_dve, v_mult_at[dc - 2])
                            P("vector", lambda e, dc=dc, t0=t0, tl=tl: e.tensor_tensor(
                                out=tmp2_f[dc % 2][:, 0:tl], in0=xbuf_t[:, dc * T + t0: dc * T + t0 + tl],
                                in1=ps_bm[:, 0:tl], op=ALU.subtract).then_inc(s_dve.h, 1))
                            v_sub_at[dc] = s_dve.inc()
                            if dc >= 1:
                                _norm_mult(dc - 1)
                        _norm_mult(DC - 1)
                        norm_done[0] = s_dve.v

            with contextlib.ExitStack() as stE:
                EE = stE.enter_context
                ybuf = EE(nc.sbuf_tensor("ybuf", [128, DC * T], BF16))
                wout_sb = [EE(nc.sbuf_tensor(f"wout_sb{i}", [128, DC * 512], BF16)) for i in range(2)]
                xres_sb = [EE(nc.sbuf_tensor(f"xres_sb{i}", [128, T], F32)) for i in range(2)]
                ps_o = [EE(nc.psum_tensor(f"ps_o{i}", [128, 512], F32)) for i in range(2)]

                W("sync", s_cc, 1)
                P("sync", lambda e: e.dma_start(
                    out=bass.AP(ybuf, 0, [[DC * T, 128], [2 * T, DC // 2], [1, T]]),
                    in_=bass.AP(a2a_out[0], 0, [[T, 128], [128 * T, DC // 2], [1, T]])).then_inc(sy_ld.h, 16))
                sy_ld.inc(16)
                W("sync", s_cc, 2)
                P("sync", lambda e: e.dma_start(
                    out=bass.AP(ybuf, T, [[DC * T, 128], [2 * T, DC // 2], [1, T]]),
                    in_=bass.AP(a2a_out[1], 0, [[T, 128], [128 * T, DC // 2], [1, T]])).then_inc(sy_ld.h, 16))
                v_yb = sy_ld.inc(16)

                xres_n = [0, 0]
                xres_cons = {}
                evac_lin = {}

                def load_xres(fc):
                    slot = fc % 2
                    if fc >= 2:
                        W("sync", s_dve, xres_cons[fc - 2])
                    P("sync", lambda e, fc=fc, slot=slot: e.dma_start(
                        out=xres_sb[slot][:], in_=xres[:, fc * T:(fc + 1) * T]).then_inc(sy_r[slot].h, 16))
                    xres_n[slot] += 1
                    sy_r[slot].inc(16)

                load_xres(0)
                if DC > 1:
                    load_xres(1)

                v_wo_half = {}
                mm_done_q = {}

                def load_wo(fh):
                    if fh >= 2:
                        W("gpsimd", s_pe, mm_done_q[fh - 2])
                    P("gpsimd", lambda e, fh=fh: e.dma_start(
                        out=wout_sb[fh % 2][:], in_=wo[fh]).then_inc(sq_wo[fh % 2].h, 16))
                    v_wo_half[fh] = sq_wo[fh % 2].inc(16)

                load_wo(0)
                load_wo(1)
                for fh in range(4):
                    for fc_l in range(4):
                        fc = fh * 4 + fc_l
                        ntsp = len(tspans)
                        for ti, (t0, tl) in enumerate(tspans):
                            li = fc * ntsp + ti
                            pslot = li % 2
                            W("tensor", sy_ld, v_yb)
                            W("tensor", sq_wo[fh % 2], v_wo_half[fh])
                            if li >= 2:
                                W("tensor", s_dve, evac_lin[li - 2])
                            for r in range(DC):
                                P("tensor", lambda e, fh=fh, fc_l=fc_l, r=r, t0=t0, tl=tl, pslot=pslot: (
                                    e.matmul(
                                        ps_o[pslot][:, 0:tl],
                                        wout_sb[fh % 2][:, r * 512 + fc_l * 128: r * 512 + fc_l * 128 + 128],
                                        ybuf[:, r * T + t0: r * T + t0 + tl],
                                        start=(r == 0), stop=(r == DC - 1)
                                    ).then_inc(s_pe.h, 1) if r == DC - 1 else e.matmul(
                                        ps_o[pslot][:, 0:tl],
                                        wout_sb[fh % 2][:, r * 512 + fc_l * 128: r * 512 + fc_l * 128 + 128],
                                        ybuf[:, r * T + t0: r * T + t0 + tl],
                                        start=(r == 0), stop=(r == DC - 1))))
                            v_mm = s_pe.inc()
                            W("vector", s_pe, v_mm)
                            W("vector", sy_r[fc % 2], 16 * xres_n[fc % 2])
                            P("vector", lambda e, fc=fc, t0=t0, tl=tl, pslot=pslot: e.tensor_tensor(
                                out=x1[:, fc * T + t0: fc * T + t0 + tl],
                                in0=ps_o[pslot][:, 0:tl], in1=xres_sb[fc % 2][:, t0:t0 + tl],
                                op=ALU.add).then_inc(s_dve.h, 1))
                            evac_lin[li] = s_dve.inc()
                        xres_cons[fc] = s_dve.v
                        if fc + 2 < DC:
                            load_xres(fc + 2)
                    mm_done_q[fh] = s_pe.v
                    if fh + 2 < 4:
                        load_wo(fh + 2)

                plan.barrier(ALL_SEMS)

            emit_layernorm(x1, x1_bf, "a")
            plan.barrier(ALL_SEMS)

            comb_all = E(nc.sbuf_tensor("comb_all", [128, NT8 * NEXP], F32))
            with contextlib.ExitStack() as stG:
                EG = stG.enter_context
                ps_s8 = [EG(nc.psum_tensor(f"ps_s8{i}", [128, NEXP], F32)) for i in range(2)]
                sc_f = EG(nc.sbuf_tensor("sc_f", [128, NEXP], F32))
                sc_t = EG(nc.sbuf_tensor("sc_t", [128, NEXP], F32))
                sc_m = EG(nc.sbuf_tensor("sc_m", [128, 1], F32))
                sc_m2 = EG(nc.sbuf_tensor("sc_m2", [128, 1], F32))

                s8_cons = {}
                for tt8 in range(NT8):
                    s8s = tt8 % 2
                    if tt8 >= 2:
                        W("tensor", s_dve, s8_cons[tt8 - 2])
                    for dc in range(DC):
                        P("tensor", lambda e, dc=dc, tt8=tt8, s8s=s8s: (
                            e.matmul(
                                ps_s8[s8s][:],
                                x1_bf[:, dc * T + tt8 * 128: dc * T + tt8 * 128 + 128],
                                gw_sb[:, dc * NEXP:(dc + 1) * NEXP],
                                start=(dc == 0), stop=(dc == DC - 1)
                            ).then_inc(s_pe.h, 1) if dc == DC - 1 else e.matmul(
                                ps_s8[s8s][:],
                                x1_bf[:, dc * T + tt8 * 128: dc * T + tt8 * 128 + 128],
                                gw_sb[:, dc * NEXP:(dc + 1) * NEXP],
                                start=(dc == 0), stop=(dc == DC - 1))))
                    v_g8 = s_pe.inc()
                    W("vector", s_pe, v_g8)

                    def CH(fn):
                        W("vector", s_dve, s_dve.v)
                        P("vector", lambda e, fn=fn: fn(e).then_inc(s_dve.h, 1))
                        s_dve.inc()

                    CH(lambda e, s8s=s8s: e.reduce_max(sc_m[:], ps_s8[s8s][:], axis=AX.X))
                    CH(lambda e, s8s=s8s: e.tensor_single_scalar(
                        out=sc_f[:], in_=ps_s8[s8s][:], scalar=sc_m[:, 0:1], op=ALU.subtract))
                    v_sub = s_dve.v
                    s8_cons[tt8] = v_sub
                    W("scalar", s_dve, v_sub)
                    P("scalar", lambda e: e.activation(out=sc_f[:], in_=sc_f[:], func=AF.Exp,
                                                       scale=1.0, bias=0.0).then_inc(s_act.h, 1))
                    v_exp = s_act.inc()
                    W("vector", s_act, v_exp)
                    CH(lambda e: e.reduce_sum(sc_m[:], sc_f[:], axis=AX.X))
                    CH(lambda e: e.reciprocal(sc_m[:], sc_m[:]))
                    CH(lambda e: e.tensor_single_scalar(
                        out=sc_f[:], in_=sc_f[:], scalar=sc_m[:, 0:1], op=ALU.mult))
                    CH(lambda e: e.reduce_max(sc_m[:], sc_f[:], axis=AX.X))
                    CH(lambda e: e.tensor_single_scalar(
                        out=sc_t[:], in_=sc_f[:], scalar=sc_m[:, 0:1], op=ALU.is_equal))
                    CH(lambda e: e.scalar_tensor_tensor(
                        out=sc_t[:], in0=sc_t[:], scalar=-2.0, in1=sc_f[:],
                        op0=ALU.mult, op1=ALU.add))
                    CH(lambda e: e.reduce_max(sc_m2[:], sc_t[:], axis=AX.X))
                    CH(lambda e: e.tensor_single_scalar(
                        out=sc_t[:], in_=sc_f[:], scalar=sc_m2[:, 0:1], op=ALU.is_ge))
                    CH(lambda e, tt8=tt8: e.tensor_tensor(
                        out=comb_all[:, tt8 * NEXP:(tt8 + 1) * NEXP],
                        in0=sc_f[:], in1=sc_t[:], op=ALU.mult))

                plan.barrier(ALL_SEMS)

            with contextlib.ExitStack() as stF:
                EF = stF.enter_context
                G_sb = EF(nc.sbuf_tensor("G_sb", [128, HE * T], BF16))
                wgu_sb = [EF(nc.sbuf_tensor(f"wgu_sb{i}", [128, DC * 128], BF16)) for i in range(4)]
                wd_sb = EF(nc.sbuf_tensor("wd_sb", [128, HE * DM], BF16))
                combT = EF(nc.sbuf_tensor("combT", [1, T], F32))
                comb_bc = EF(nc.sbuf_tensor("comb_bc", [128, T], F32))
                sg_sb = [EF(nc.sbuf_tensor(f"sg_sb{i}", [128, 512], BF16)) for i in range(2)]
                u2_f = [EF(nc.sbuf_tensor(f"u2_f{i}", [128, 512], F32)) for i in range(2)]
                ps_gu = [EF(nc.psum_tensor(f"ps_gu{i}", [128, 512], F32)) for i in range(4)]
                ps_dn = [EF(nc.psum_tensor(f"ps_dn{i}", [128, 512], F32)) for i in range(2)]

                passes = [(0, ei) for ei in range(NEXP)] + [(1, 0), (1, 1)]
                nchunks = len(passes) * HE
                chunk_cons = {}
                load_n = [0, 0, 0, 0]
                load_ord = [0]
                dn_mm_done = {}

                def chunk_src(ci):
                    pi, h = divmod(ci, HE)
                    kind, ei = passes[pi]
                    if kind == 0:
                        return wgate[ei, h], wup[ei, h]
                    return swg[ei * HE + h], swu[ei * HE + h]

                def load_gu():
                    ci = load_ord[0]
                    load_ord[0] += 1
                    gslot = ci % 2
                    if ci >= 2 and (ci - 2) in chunk_cons:
                        cs, cv = chunk_cons[ci - 2]
                        W("gpsimd", cs, cv)
                    sg_, su_ = chunk_src(ci)
                    P("gpsimd", lambda e, s=sg_, gslot=gslot: e.dma_start(
                        out=wgu_sb[2 * gslot][:], in_=s).then_inc(sq_g[2 * gslot].h, 16))
                    load_n[2 * gslot] += 1
                    sq_g[2 * gslot].inc(16)
                    P("gpsimd", lambda e, s=su_, gslot=gslot: e.dma_start(
                        out=wgu_sb[2 * gslot + 1][:], in_=s).then_inc(sq_g[2 * gslot + 1].h, 16))
                    load_n[2 * gslot + 1] += 1
                    sq_g[2 * gslot + 1].inc(16)

                def load_wd(pi):
                    kind, ei = passes[pi]
                    if pi >= 1:
                        W("gpsimd", s_pe, dn_mm_done[pi - 1])
                    for h in range(HE):
                        src = wdown[ei, h] if kind == 0 else swd[ei * HE + h]
                        P("gpsimd", lambda e, src=src, h=h: e.dma_start(
                            out=wd_sb[:, h * DM:(h + 1) * DM], in_=src).then_inc(sq_d.h, 16))
                        sq_d.inc(16)

                load_gu()
                load_gu()
                load_wd(0)

                last_gu_cons = {}
                prev_pass_G_done = 0
                prev_pass_dn_done = 0
                for pi, (kind, ei) in enumerate(passes):
                    v_combbc = 0
                    if kind == 0:
                        v_ctcp_prev = 0
                        for ti in range(len(tspans)):
                            if ti in last_gu_cons:
                                cs, cv = last_gu_cons[ti]
                                W("tensor", cs, cv)
                        for tt8 in range(NT8):
                            if tt8 >= 1:
                                W("tensor", s_act, v_ctcp_prev)
                            P("tensor", lambda e, tt8=tt8, ei=ei: e.transpose(
                                ps_gu[1][0:1, 0:128], comb_all[:, tt8 * NEXP + ei: tt8 * NEXP + ei + 1],
                                ident_f[:]).then_inc(s_pe.h, 1))
                            v_tr = s_pe.inc()
                            W("scalar", s_pe, v_tr)
                            P("scalar", lambda e, tt8=tt8: e.copy(
                                out=combT[:, tt8 * 128:(tt8 + 1) * 128],
                                in_=ps_gu[1][0:1, 0:128]).then_inc(s_act.h, 1))
                            v_ctcp_prev = s_act.inc()
                        for ti, (t0, tl) in enumerate(tspans):
                            W("tensor", s_act, v_ctcp_prev)
                            P("tensor", lambda e, t0=t0, tl=tl, ti=ti: e.matmul(
                                ps_gu[ti][:, 0:tl], ones_row[:], combT[:, t0:t0 + tl],
                                start=True, stop=True).then_inc(s_pe.h, 1))
                            v_cbm = s_pe.inc()
                            W("scalar", s_pe, v_cbm)
                            P("scalar", lambda e, t0=t0, tl=tl, ti=ti: e.copy(
                                out=comb_bc[:, t0:t0 + tl], in_=ps_gu[ti][:, 0:tl]).then_inc(s_act.h, 1))
                            v_combbc = s_act.inc()
                            last_gu_cons[ti] = (s_act, v_combbc)

                    for h in range(HE):
                        ci = pi * HE + h
                        gslot = ci % 2
                        if pi >= 1 and h == 0:
                            W("vector", s_pe, prev_pass_dn_done)
                        for ti, (t0, tl) in enumerate(tspans):
                            pg = ps_gu[ti]
                            pu = ps_gu[2 + ti]
                            W("tensor", sq_g[2 * gslot], 16 * load_n[2 * gslot])
                            W("tensor", sq_g[2 * gslot + 1], 16 * load_n[2 * gslot + 1])
                            if ti in last_gu_cons:
                                cs, cv = last_gu_cons[ti]
                                W("tensor", cs, cv)
                            for dc in range(DC):
                                P("tensor", lambda e, gslot=gslot, dc=dc, t0=t0, tl=tl, pg=pg: (
                                    e.matmul(
                                        pg[:, 0:tl], wgu_sb[2 * gslot][:, dc * 128:(dc + 1) * 128],
                                        x1_bf[:, dc * T + t0: dc * T + t0 + tl],
                                        start=(dc == 0), stop=(dc == DC - 1)
                                    ).then_inc(s_pe.h, 1) if dc == DC - 1 else e.matmul(
                                        pg[:, 0:tl], wgu_sb[2 * gslot][:, dc * 128:(dc + 1) * 128],
                                        x1_bf[:, dc * T + t0: dc * T + t0 + tl],
                                        start=(dc == 0), stop=(dc == DC - 1))))
                            v_gd = s_pe.inc()
                            for dc in range(DC):
                                P("tensor", lambda e, gslot=gslot, dc=dc, t0=t0, tl=tl, pu=pu: (
                                    e.matmul(
                                        pu[:, 0:tl], wgu_sb[2 * gslot + 1][:, dc * 128:(dc + 1) * 128],
                                        x1_bf[:, dc * T + t0: dc * T + t0 + tl],
                                        start=(dc == 0), stop=(dc == DC - 1)
                                    ).then_inc(s_pe.h, 1) if dc == DC - 1 else e.matmul(
                                        pu[:, 0:tl], wgu_sb[2 * gslot + 1][:, dc * 128:(dc + 1) * 128],
                                        x1_bf[:, dc * T + t0: dc * T + t0 + tl],
                                        start=(dc == 0), stop=(dc == DC - 1))))
                            v_ud = s_pe.inc()
                            W("scalar", s_pe, v_gd)
                            P("scalar", lambda e, tl=tl, pg=pg, ti=ti: e.activation(
                                out=sg_sb[ti][:, 0:tl], in_=pg[:, 0:tl], func=AF.Silu,
                                scale=1.0, bias=0.0).then_inc(s_act.h, 1))
                            v_sg = s_act.inc()
                            W("vector", s_pe, v_ud)
                            W("vector", s_act, v_sg)
                            if kind == 0:
                                P("vector", lambda e, t0=t0, tl=tl, pu=pu, ti=ti: e.tensor_tensor(
                                    out=u2_f[ti][:, 0:tl], in0=pu[:, 0:tl],
                                    in1=comb_bc[:, t0:t0 + tl], op=ALU.mult).then_inc(s_dve.h, 1))
                                v_u2 = s_dve.inc()
                                W("vector", s_dve, v_u2)
                                P("vector", lambda e, h=h, t0=t0, tl=tl, ti=ti: e.tensor_tensor(
                                    out=G_sb[:, h * T + t0: h * T + t0 + tl],
                                    in0=u2_f[ti][:, 0:tl], in1=sg_sb[ti][:, 0:tl],
                                    op=ALU.mult).then_inc(s_dve.h, 1))
                            else:
                                P("vector", lambda e, h=h, t0=t0, tl=tl, pu=pu, ti=ti: e.tensor_tensor(
                                    out=G_sb[:, h * T + t0: h * T + t0 + tl],
                                    in0=pu[:, 0:tl], in1=sg_sb[ti][:, 0:tl],
                                    op=ALU.mult).then_inc(s_dve.h, 1))
                            vv = s_dve.inc()
                            last_gu_cons[ti] = (s_dve, vv)
                            chunk_cons[ci] = (s_dve, vv)
                        if load_ord[0] < nchunks:
                            load_gu()
                    v_G_all = s_dve.v

                    W("tensor", sq_d, sq_d.v)
                    W("tensor", s_dve, v_G_all)
                    dn_evac = {}
                    ntsp = len(tspans)
                    for fc in range(DC):
                        for ti, (t0, tl) in enumerate(tspans):
                            di = fc * ntsp + ti
                            dslot = di % 2
                            if di >= 2:
                                W("tensor", s_dve, dn_evac[di - 2])
                            for h in range(HE):
                                P("tensor", lambda e, h=h, fc=fc, t0=t0, tl=tl, dslot=dslot: (
                                    e.matmul(
                                        ps_dn[dslot][:, 0:tl],
                                        wd_sb[:, h * DM + fc * 128: h * DM + fc * 128 + 128],
                                        G_sb[:, h * T + t0: h * T + t0 + tl],
                                        start=(h == 0), stop=(h == HE - 1)
                                    ).then_inc(s_pe.h, 1) if h == HE - 1 else e.matmul(
                                        ps_dn[dslot][:, 0:tl],
                                        wd_sb[:, h * DM + fc * 128: h * DM + fc * 128 + 128],
                                        G_sb[:, h * T + t0: h * T + t0 + tl],
                                        start=(h == 0), stop=(h == HE - 1))))
                            v_dn = s_pe.inc()
                            W("vector", s_pe, v_dn)
                            P("vector", lambda e, fc=fc, t0=t0, tl=tl, dslot=dslot: e.tensor_tensor(
                                out=x1[:, fc * T + t0: fc * T + t0 + tl],
                                in0=ps_dn[dslot][:, 0:tl],
                                in1=x1[:, fc * T + t0: fc * T + t0 + tl],
                                op=ALU.add).then_inc(s_dve.h, 1))
                            dn_evac[di] = s_dve.inc()
                    dn_mm_done[pi] = s_pe.v
                    prev_pass_dn_done = s_pe.v
                    if pi + 1 < len(passes):
                        load_wd(pi + 1)

                plan.barrier(ALL_SEMS)

            def _store_chunk(dc, v_n):
                W("sync", s_dve, v_n)
                P("sync", lambda e, dc=dc: e.dma_start(
                    out=out_d[:, dc * T:(dc + 1) * T],
                    in_=x1[:, dc * T:(dc + 1) * T]).then_inc(sy_st.h, 16))
                sy_st.inc(16)

            emit_layernorm(x1, None, "b", store_cb=_store_chunk)
            plan.barrier(ALL_SEMS)
            fin = sy_st.v
            P("sync", lambda e, fin=fin: e.wait_ge(sy_st.h, fin))

        with nc.Block() as blk:
            @blk.tensor
            def _(eng):
                for f in plan.ops["tensor"]:
                    f(eng)

            @blk.vector
            def _(eng):
                for f in plan.ops["vector"]:
                    f(eng)

            @blk.scalar
            def _(eng):
                for f in plan.ops["scalar"]:
                    f(eng)

            @blk.gpsimd
            def _(eng):
                for f in plan.ops["gpsimd"]:
                    f(eng)

            @blk.sync
            def _(eng):
                for f in plan.ops["sync"]:
                    f(eng)

    return nc



def _prep_inputs(S, L0_attn_x, L0_attn_w_qkv, L0_attn_w_out, L0_attn_q_gain,
                 L0_ffn_gate_w, L0_ffn_w_gate, L0_ffn_w_up, L0_ffn_w_down,
                 L0_ffn_sw_gate, L0_ffn_sw_up, L0_ffn_sw_down):
    T = S // NCORES
    NT = S // 512
    f32 = np.float32
    C = np.ascontiguousarray
    x2d = np.asarray(L0_attn_x, dtype=f32).reshape(S, DM)
    wqkv = np.asarray(L0_attn_w_qkv, dtype=f32)
    w_out = np.asarray(L0_attn_w_out, dtype=f32)
    q_gain = np.asarray(L0_attn_q_gain, dtype=f32)
    gate_w = np.asarray(L0_ffn_gate_w, dtype=f32)
    w_gate = np.asarray(L0_ffn_w_gate, dtype=f32)
    w_up = np.asarray(L0_ffn_w_up, dtype=f32)
    w_down = np.asarray(L0_ffn_w_down, dtype=f32)
    sw_gate = np.asarray(L0_ffn_sw_gate, dtype=f32)
    sw_up = np.asarray(L0_ffn_sw_up, dtype=f32)
    sw_down = np.asarray(L0_ffn_sw_down, dtype=f32)

    xt = C(x2d.reshape(NT, 512, DC, 128).transpose(0, 3, 2, 1).reshape(NT, 128, DC * 512))
    wo_t = C(w_out.reshape(DC, 128, 4, 512).transpose(2, 1, 0, 3).reshape(4, 128, DC * 512))
    gw_t = C(gate_w.reshape(DC, 128, NEXP).transpose(1, 0, 2).reshape(128, DC * NEXP))
    wg_t = C(w_gate.reshape(NEXP, DC, 128, HE, 128).transpose(0, 3, 2, 1, 4).reshape(NEXP, HE, 128, DC * 128))
    wu_t = C(w_up.reshape(NEXP, DC, 128, HE, 128).transpose(0, 3, 2, 1, 4).reshape(NEXP, HE, 128, DC * 128))
    wd_t = C(w_down.reshape(NEXP, HE, 128, DM))
    swg_t = C(sw_gate.reshape(DC, 128, HS2, 128).transpose(2, 1, 0, 3).reshape(HS2, 128, DC * 128))
    swu_t = C(sw_up.reshape(DC, 128, HS2, 128).transpose(2, 1, 0, 3).reshape(HS2, 128, DC * 128))
    swd_t = C(sw_down.reshape(HS2, 128, DM))

    in_maps = []
    for c in range(NCORES):
        def wslice(base):
            cols = slice(base + 2 * c * 128, base + 2 * c * 128 + 256)
            return C(wqkv[:, cols].reshape(DC, 128, 256).transpose(1, 0, 2).reshape(128, DC * 256))

        xres_c = C(x2d[c * T:(c + 1) * T].reshape(T, DC, 128).transpose(2, 1, 0).reshape(128, DC * T))
        in_maps.append({
            "xt": xt,
            "xres": xres_c,
            "wq": wslice(0),
            "wk": wslice(DM),
            "wv": wslice(2 * DM),
            "wo": wo_t,
            "qg": C(q_gain[2 * c:2 * c + 2].reshape(1, 2)),
            "gw": gw_t,
            "wgate": wg_t,
            "wup": wu_t,
            "wdown": wd_t,
            "swg": swg_t,
            "swu": swu_t,
            "swd": swd_t,
        })
    return in_maps


def _assemble(S, results):
    T = S // NCORES
    full = np.empty((1, S, DM), dtype=np.float32)
    for c in range(NCORES):
        arr = np.asarray(results[c]["out"]).reshape(128, DC, T)
        full[0, c * T:(c + 1) * T, :] = arr.transpose(2, 1, 0).reshape(T, DM)
    return full


def kernel(L0_attn_x, L0_attn_w_qkv, L0_attn_w_out, L0_attn_q_gain,
           L0_ffn_gate_w, L0_ffn_w_gate, L0_ffn_w_up, L0_ffn_w_down,
           L0_ffn_sw_gate, L0_ffn_sw_up, L0_ffn_sw_down):
    S = int(np.asarray(L0_attn_x).shape[1])
    in_maps = _prep_inputs(S, L0_attn_x, L0_attn_w_qkv, L0_attn_w_out,
                           L0_attn_q_gain, L0_ffn_gate_w, L0_ffn_w_gate,
                           L0_ffn_w_up, L0_ffn_w_down, L0_ffn_sw_gate,
                           L0_ffn_sw_up, L0_ffn_sw_down)
    nc = build_nc(S)
    res = run_bass_kernel_spmd(nc, in_maps, core_ids=list(range(NCORES)))
    return _assemble(S, res.results)


# version 16
# speedup vs baseline: 1.2333x; 1.2333x over previous
import math
import contextlib

import numpy as np

import concourse.bass as bass
import concourse.mybir as mybir
from concourse.bass_utils import run_bass_kernel_spmd
from concourse.aot_env import allow_ici_all_to_all

F32 = mybir.dt.float32
BF16 = mybir.dt.bfloat16
AF = mybir.ActivationFunctionType
ALU = mybir.AluOpType
AX = mybir.AxisListType

NCORES = 8
DM = 2048
DC = DM // 128
HFF = 1408
HE = HFF // 128
HSH = 2816
HS2 = HSH // 128
NEXP = 8
LN_EPS = 1e-5
UN_EPS = 1e-6
ROPE_BASE = 10000.0
PI = math.pi
TWO_PI = 2.0 * math.pi


class Sem:
    def __init__(self, nc, st, name):
        self.h = st.enter_context(nc.semaphore(name))
        self.v = 0

    def inc(self, n=1):
        self.v += n
        return self.v


class Plan:
    ENGINES = ("tensor", "vector", "scalar", "gpsimd", "sync")

    def __init__(self):
        self.ops = {e: [] for e in self.ENGINES}

    def add(self, eng, fn):
        self.ops[eng].append(fn)

    def wait(self, eng, sem, val):
        if val > 0:
            self.ops[eng].append(lambda e, s=sem.h, v=val: e.wait_ge(s, v))

    def barrier(self, sems):
        for eng in self.ENGINES:
            for s in sems:
                self.wait(eng, s, s.v)


def build_nc(S):
    T = S // NCORES
    NT = S // 512
    NQ = S // 512
    NT8 = T // 128
    tsp = 512 if T >= 1024 else max(64, T // 2)
    tspans = [(i * tsp, min(tsp, T - i * tsp)) for i in range(max(1, T // tsp))]

    nc = bass.Bass("TRN2", target_bir_lowering=False)

    xt = nc.dram_tensor("xt", [NT, 128, DC * 512], F32, kind="ExternalInput")
    xres = nc.dram_tensor("xres", [128, DC * T], F32, kind="ExternalInput")
    wq = nc.dram_tensor("wq", [128, DC * 256], F32, kind="ExternalInput")
    wk = nc.dram_tensor("wk", [128, DC * 256], F32, kind="ExternalInput")
    wv = nc.dram_tensor("wv", [128, DC * 256], F32, kind="ExternalInput")
    wo = nc.dram_tensor("wo", [4, 128, DC * 512], F32, kind="ExternalInput")
    qg = nc.dram_tensor("qg", [1, 2], F32, kind="ExternalInput")
    gw = nc.dram_tensor("gw", [128, DC * NEXP], F32, kind="ExternalInput")
    wgate = nc.dram_tensor("wgate", [NEXP, HE, 128, DC * 128], F32, kind="ExternalInput")
    wup = nc.dram_tensor("wup", [NEXP, HE, 128, DC * 128], F32, kind="ExternalInput")
    wdown = nc.dram_tensor("wdown", [NEXP, HE, 128, DM], F32, kind="ExternalInput")
    swg = nc.dram_tensor("swg", [HS2, 128, DC * 128], F32, kind="ExternalInput")
    swu = nc.dram_tensor("swu", [HS2, 128, DC * 128], F32, kind="ExternalInput")
    swd = nc.dram_tensor("swd", [HS2, 128, DM], F32, kind="ExternalInput")
    out_d = nc.dram_tensor("out", [128, DC * T], F32, kind="ExternalOutput")
    a2a_in = [nc.dram_tensor(f"a2a_in{h}", [NCORES, 128, T], BF16) for h in range(2)]
    a2a_out = [nc.dram_tensor(f"a2a_out{h}", [NCORES, 128, T], BF16) for h in range(2)]

    st = contextlib.ExitStack()
    with st:
        E = st.enter_context

        sq_w = Sem(nc, st, "sq_w")
        sq_x = [Sem(nc, st, f"sq_x{i}") for i in range(2)]
        sq_g = [Sem(nc, st, f"sq_g{i}") for i in range(4)]
        sq_d = Sem(nc, st, "sq_d")
        sy_r = [Sem(nc, st, f"sy_r{i}") for i in range(2)]
        sq_wo = [Sem(nc, st, f"sq_wo{i}") for i in range(2)]
        sy_ld = Sem(nc, st, "sy_ld")
        sy_sh = Sem(nc, st, "sy_sh")
        sy_st = Sem(nc, st, "sy_st")
        s_pe = Sem(nc, st, "s_pe")
        s_act = Sem(nc, st, "s_act")
        s_dve = Sem(nc, st, "s_dve")
        s_gp = Sem(nc, st, "s_gp")
        s_cc = Sem(nc, st, "s_cc")
        ALL_SEMS = ([sq_w] + sq_x + sq_g + [sq_d] + sy_r + sq_wo +
                    [sy_ld, sy_sh, sy_st, s_pe, s_act, s_dve, s_gp, s_cc])

        ones_row = E(nc.sbuf_tensor("ones_row", [1, 128], F32))
        ones_cb = E(nc.sbuf_tensor("ones_cb", [128, 1], BF16))
        ones_cf = E(nc.sbuf_tensor("ones_cf", [128, 1], F32))
        ident_b = E(nc.sbuf_tensor("ident_b", [128, 128], BF16))
        ident_f = E(nc.sbuf_tensor("ident_f", [128, 128], F32))
        perm_a = E(nc.sbuf_tensor("perm_a", [128, 128], F32))
        perm_b = E(nc.sbuf_tensor("perm_b", [128, 128], BF16))
        qg_sb = E(nc.sbuf_tensor("qg_sb", [1, 2], F32))
        gcol = E(nc.sbuf_tensor("gcol", [128, 2], F32))
        gneg = E(nc.sbuf_tensor("gneg", [128, 2], F32))
        masks = E(nc.sbuf_tensor("masks", [128, 4 * 512], BF16))
        fcol = E(nc.sbuf_tensor("fcol", [64, 1], F32))
        f2p = E(nc.sbuf_tensor("f2p", [64, 1], F32))
        jcol = E(nc.sbuf_tensor("jcol", [64, 1], F32))
        gw_sb = E(nc.sbuf_tensor("gw_sb", [128, DC * NEXP], BF16))
        eps6 = E(nc.sbuf_tensor("eps6", [1, 1], F32))
        eps5 = E(nc.sbuf_tensor("eps5", [1, 1], F32))

        plan = Plan()
        P = plan.add
        W = plan.wait

        P("sync", lambda e: e.dma_start(out=qg_sb[:], in_=qg[:]).then_inc(sy_ld.h, 16))
        v_qg = sy_ld.inc(16)
        P("gpsimd", lambda e: e.dma_start(out=gw_sb[:], in_=gw[:]).then_inc(sq_w.h, 16))
        sq_w.inc(16)

        P("vector", lambda e: e.memset(eps6[:], UN_EPS))
        P("vector", lambda e: e.memset(eps5[:], LN_EPS))
        P("vector", lambda e: e.memset(ones_row[:], 1.0))
        P("vector", lambda e: e.memset(ones_cb[:], 1.0))
        P("vector", lambda e: e.memset(ones_cf[:], 1.0).then_inc(s_dve.h, 1))
        v_ones = s_dve.inc()

        for tens in (ident_b, ident_f):
            P("gpsimd", lambda e, t=tens: e.memset(t[:], 1.0).then_inc(s_gp.h, 1))
            v0 = s_gp.inc()
            W("gpsimd", s_gp, v0)
            P("gpsimd", lambda e, t=tens: e.affine_select(
                out=t[:], in_=t[:], pattern=[[1, 128]],
                compare_op=ALU.is_equal, fill=0.0, base=0,
                channel_multiplier=-1).then_inc(s_gp.h, 1))
            s_gp.inc()
        for tens, base in ((perm_a, -64), (perm_b, 64)):
            P("gpsimd", lambda e, t=tens: e.memset(t[:], 1.0).then_inc(s_gp.h, 1))
            v0 = s_gp.inc()
            W("gpsimd", s_gp, v0)
            P("gpsimd", lambda e, t=tens, base=base: e.affine_select(
                out=t[:], in_=t[:], pattern=[[1, 128]],
                compare_op=ALU.is_equal, fill=0.0, base=base,
                channel_multiplier=-1).then_inc(s_gp.h, 1))
            s_gp.inc()
        v_perm_gp = s_gp.v
        W("vector", s_gp, v_perm_gp)
        P("vector", lambda e: e.tensor_tensor(out=perm_a[:], in0=perm_a[:], in1=perm_b[:], op=ALU.add).then_inc(s_dve.h, 1))
        v_perm = s_dve.inc()
        P("gpsimd", lambda e: e.memset(masks[:], 1.0).then_inc(s_gp.h, 1))
        v0 = s_gp.inc()
        W("gpsimd", s_gp, v0)
        for j in range(4):
            P("gpsimd", lambda e, j=j: e.affine_select(
                out=masks[:, j * 512:(j + 1) * 512],
                in_=masks[:, j * 512:(j + 1) * 512], pattern=[[1, 512]],
                compare_op=ALU.is_ge, fill=0.0, base=-128 * j,
                channel_multiplier=-1).then_inc(s_gp.h, 1))
            s_gp.inc()
        v_masks = s_gp.v

        P("gpsimd", lambda e: e.iota(
            jcol[:], [[1, 1]], base=0, channel_multiplier=1,
            allow_small_or_imprecise_dtypes=True).then_inc(s_gp.h, 1))
        v_j = s_gp.inc()
        W("scalar", s_gp, v_j)
        P("scalar", lambda e: e.activation(
            out=fcol[:], in_=jcol[:], func=AF.Exp,
            scale=-math.log(ROPE_BASE) / 64.0, bias=0.0).then_inc(s_act.h, 1))
        v_f0 = s_act.inc()
        W("vector", s_act, v_f0)
        P("vector", lambda e: e.tensor_single_scalar(
            out=f2p[:], in_=fcol[:], scalar=1.0 / TWO_PI, op=ALU.mult).then_inc(s_dve.h, 1))
        v_f = s_dve.inc()

        with contextlib.ExitStack() as st0:
            ps_g = st0.enter_context(nc.psum_tensor("ps_g", [128, 2], F32))
            W("tensor", sy_ld, v_qg)
            W("tensor", s_dve, v_ones)
            P("tensor", lambda e: e.matmul(ps_g[:], ones_row[:], qg_sb[:],
                                           start=True, stop=True).then_inc(s_pe.h, 1))
            v0 = s_pe.inc()
            W("scalar", s_pe, v0)
            P("scalar", lambda e: e.copy(out=gcol[:], in_=ps_g[:]).then_inc(s_act.h, 1))
            s_act.inc()
            P("scalar", lambda e: e.activation(out=gneg[:], in_=ps_g[:], func=AF.Abs,
                                               scale=1.0, bias=0.0).then_inc(s_act.h, 1))
            v0 = s_act.inc()
            W("vector", s_act, v0)
            P("vector", lambda e: e.tensor_single_scalar(
                out=gneg[:], in_=gneg[:], scalar=-1.0, op=ALU.mult).then_inc(s_dve.h, 1))
            s_dve.inc()
            plan.barrier([s_pe, s_act, s_dve])

            with contextlib.ExitStack() as stBC:
                EBC = stBC.enter_context
                QT = EBC(nc.sbuf_tensor("QT", [128, 2 * S], BF16))
                KT = EBC(nc.sbuf_tensor("KT", [128, 2 * S], BF16))
                VT = EBC(nc.sbuf_tensor("VT", [128, 2 * S], BF16))

                with contextlib.ExitStack() as stB:
                    EB = stB.enter_context
                    wq_sb = EB(nc.sbuf_tensor("wq_sb", [128, DC * 256], BF16))
                    wk_sb = EB(nc.sbuf_tensor("wk_sb", [128, DC * 256], BF16))
                    wv_sb = EB(nc.sbuf_tensor("wv_sb", [128, DC * 256], BF16))
                    xbuf = [EB(nc.sbuf_tensor(f"xbuf{i}", [128, DC * 512], BF16)) for i in range(2)]
                    ps_qkv = [EB(nc.psum_tensor(f"ps_qkv{i}", [128, 512], F32)) for i in range(4)]
                    ps_ss = EB(nc.psum_tensor("ps_ss", [1, 512], F32))
                    ps_bc = [EB(nc.psum_tensor("ps_bc0", [128, 512], F32))] * 2
                    ps_vt = EB(nc.psum_tensor("ps_vt", [128, 512], BF16))
                    sq_b = [EB(nc.sbuf_tensor(f"sq_b{i}", [128, 512], BF16)) for i in range(2)]
                    rn_r = EB(nc.sbuf_tensor("rn_r", [1, 512], F32))
                    rn2_r = EB(nc.sbuf_tensor("rn2_r", [1, 512], F32))
                    bc_sb = [EB(nc.sbuf_tensor(f"bc_sb{i}", [128, 512], F32)) for i in range(2)]
                    qn_f = EB(nc.sbuf_tensor("qn_f", [128, 512], F32))
                    t64a = EB(nc.sbuf_tensor("t64a", [64, 512], F32))
                    t64b = EB(nc.sbuf_tensor("t64b", [64, 512], F32))
                    t64c = EB(nc.sbuf_tensor("t64c", [64, 512], F32))
                    t64d = EB(nc.sbuf_tensor("t64d", [64, 512], F32))
                    v_sb = [EB(nc.sbuf_tensor(f"v_sb{i}", [128, 512], BF16)) for i in range(2)]
                    pos_t = EB(nc.sbuf_tensor("pos_t", [64, 512], F32))
                    th_t = EB(nc.sbuf_tensor("th_t", [64, 512], F32))
                    th2_t = EB(nc.sbuf_tensor("th2_t", [64, 512], F32))
                    m_t = EB(nc.sbuf_tensor("m_t", [64, 512], F32))
                    m2_t = EB(nc.sbuf_tensor("m2_t", [64, 512], F32))
                    k_t = EB(nc.sbuf_tensor("k_t", [64, 512], F32))
                    k2_t = EB(nc.sbuf_tensor("k2_t", [64, 512], F32))
                    cosT = [EB(nc.sbuf_tensor(f"cosT{i}", [64, 512], F32)) for i in range(2)]
                    sinT = [EB(nc.sbuf_tensor(f"sinT{i}", [64, 512], F32)) for i in range(2)]
                    i32a = EB(nc.sbuf_tensor("i32a", [64, 512], mybir.dt.int32))
                    i32b = EB(nc.sbuf_tensor("i32b", [64, 512], mybir.dt.int32))

                    for w_d, w_s in ((wq, wq_sb), (wk, wk_sb), (wv, wv_sb)):
                        P("gpsimd", lambda e, a=w_d, b=w_s: e.dma_start(
                            out=b[:], in_=a[:]).then_inc(sq_w.h, 16))
                        sq_w.inc(16)
                    v_wqkv = sq_w.v

                    xload = [0, 0]
                    pe_at_tt_end = []
                    dve_at_tt_end = []
                    grp_cons = {}
                    bc_cons = {}
                    bc_bank_cons = {}
                    vtm_done = [None]
                    qk_count = [0]
                    ss_done = {}
                    sqrt_done = {}
                    recip_done = {}
                    bcmm_done = {}
                    vcopy_slot_cons = {}
                    tr_done = {}
                    swap_done = [0]

                    def load_xtile(tt):
                        slot = tt % 2
                        if tt >= 2:
                            W("gpsimd", s_pe, pe_at_tt_end[tt - 2])
                        P("gpsimd", lambda e, tt=tt, slot=slot: e.dma_start(
                            out=xbuf[slot][:], in_=xt[tt]).then_inc(sq_x[slot].h, 16))
                        xload[slot] += 1
                        sq_x[slot].inc(16)

                    load_xtile(0)
                    if NT > 1:
                        load_xtile(1)

                    gidx = 0
                    v_tab_prev = 0
                    v_L1_prev = 0
                    post0 = {}
                    post1 = {}
                    post2 = {}
                    for tt in range(NT):
                        slot = tt % 2
                        tslot = tt % 2
                        if tt >= 1:
                            W("gpsimd", s_dve, v_L1_prev)
                        P("gpsimd", lambda e, tt=tt: e.iota(
                            pos_t[:], [[1, 512]], base=tt * 512, channel_multiplier=0,
                            allow_small_or_imprecise_dtypes=True).then_inc(s_gp.h, 1))
                        v_pos = s_gp.inc()
                        W("vector", s_gp, v_pos)
                        W("vector", s_dve, v_f)
                        if tt >= 1:
                            W("vector", s_act, v_tab_prev)
                        P("vector", lambda e: e.tensor_single_scalar(out=th_t[:], in_=pos_t[:], scalar=fcol[:, 0:1], op=ALU.mult))
                        P("vector", lambda e: e.tensor_scalar(
                            out=th2_t[:], in0=pos_t[:], scalar1=fcol[:, 0:1],
                            scalar2=PI / 2.0, op0=ALU.mult, op1=ALU.add))
                        P("vector", lambda e: e.tensor_single_scalar(out=m_t[:], in_=pos_t[:], scalar=f2p[:, 0:1], op=ALU.mult))
                        P("vector", lambda e: e.tensor_scalar(
                            out=m2_t[:], in0=pos_t[:], scalar1=f2p[:, 0:1],
                            scalar2=0.25, op0=ALU.mult, op1=ALU.add).then_inc(s_dve.h, 1))
                        v_L1_prev = s_dve.inc()
                        W("vector", s_dve, v_L1_prev)
                        P("vector", lambda e: e.tensor_copy(out=i32a[:], in_=m_t[:]))
                        P("vector", lambda e: e.tensor_copy(out=i32b[:], in_=m2_t[:]).then_inc(s_dve.h, 1))
                        W("vector", s_dve, s_dve.inc())
                        P("vector", lambda e: e.tensor_copy(out=k_t[:], in_=i32a[:]))
                        P("vector", lambda e: e.tensor_copy(out=k2_t[:], in_=i32b[:]).then_inc(s_dve.h, 1))
                        W("vector", s_dve, s_dve.inc())
                        P("vector", lambda e: e.scalar_tensor_tensor(
                            out=m_t[:], in0=k_t[:], scalar=-TWO_PI, in1=th_t[:],
                            op0=ALU.mult, op1=ALU.add))
                        P("vector", lambda e: e.scalar_tensor_tensor(
                            out=m2_t[:], in0=k2_t[:], scalar=-TWO_PI, in1=th2_t[:],
                            op0=ALU.mult, op1=ALU.add).then_inc(s_dve.h, 1))
                        W("vector", s_dve, s_dve.inc())
                        P("vector", lambda e: e.tensor_single_scalar(out=k_t[:], in_=m_t[:], scalar=PI, op=ALU.is_ge))
                        P("vector", lambda e: e.tensor_single_scalar(out=k2_t[:], in_=m2_t[:], scalar=PI, op=ALU.is_ge).then_inc(s_dve.h, 1))
                        W("vector", s_dve, s_dve.inc())
                        P("vector", lambda e: e.scalar_tensor_tensor(
                            out=th_t[:], in0=k_t[:], scalar=-TWO_PI, in1=m_t[:],
                            op0=ALU.mult, op1=ALU.add))
                        P("vector", lambda e: e.scalar_tensor_tensor(
                            out=th2_t[:], in0=k2_t[:], scalar=-TWO_PI, in1=m2_t[:],
                            op0=ALU.mult, op1=ALU.add).then_inc(s_dve.h, 1))
                        v_rs = s_dve.inc()
                        W("scalar", s_dve, v_rs)
                        P("scalar", lambda e, tslot=tslot: e.activation(out=sinT[tslot][:], in_=th_t[:], func=AF.Sin,
                                                           scale=1.0, bias=0.0))
                        P("scalar", lambda e, tslot=tslot: e.activation(out=cosT[tslot][:], in_=th2_t[:], func=AF.Sin,
                                                           scale=1.0, bias=0.0).then_inc(s_act.h, 1))
                        v_tab = s_act.inc()
                        v_tab_prev = v_tab

                        groups = [(wq_sb, 0, 0), (wq_sb, 1, 0),
                                  (wk_sb, 0, 1), (wk_sb, 1, 1),
                                  (wv_sb, 0, 2), (wv_sb, 1, 2)]
                        for (wsb, hs, kind) in groups:
                            g = gidx
                            if (g - 3) in post2:
                                post2.pop(g - 3)()
                            if (g - 2) in post1:
                                post1.pop(g - 2)()
                            if (g - 1) in post0:
                                post0.pop(g - 1)()
                            pslot = g % 4
                            qk_i = qk_count[0] if kind < 2 else None
                            W("tensor", sq_x[slot], 16 * xload[slot])
                            if g == 0:
                                W("tensor", sq_w, v_wqkv)
                                W("tensor", s_gp, v_masks)
                            if (g - 4) in grp_cons:
                                cs, cv = grp_cons.pop(g - 4)
                                W("tensor", cs, cv)
                            for dc in range(DC):
                                if dc < DC - 1:
                                    P("tensor", lambda e, wsb=wsb, hs=hs, dc=dc, pslot=pslot, slot=slot: e.matmul(
                                        ps_qkv[pslot][:],
                                        wsb[:, dc * 256 + hs * 128: dc * 256 + hs * 128 + 128],
                                        xbuf[slot][:, dc * 512: dc * 512 + 512],
                                        start=(dc == 0), stop=False))
                                else:
                                    P("tensor", lambda e, wsb=wsb, hs=hs, dc=dc, pslot=pslot, slot=slot: e.matmul(
                                        ps_qkv[pslot][:],
                                        wsb[:, dc * 256 + hs * 128: dc * 256 + hs * 128 + 128],
                                        xbuf[slot][:, dc * 512: dc * 512 + 512],
                                        start=False, stop=True).then_inc(s_pe.h, 1))
                            v_mm = s_pe.inc()

                            if kind < 2:
                                qk_count[0] += 1
                                sqs = qk_i % 2
                                v_sq_box = [None]

                                def mk_post0_qk(g=g, qk_i=qk_i, sqs=sqs, pslot=pslot, v_mm=v_mm, v_sq_box=v_sq_box):
                                    def f():
                                        W("scalar", s_pe, v_mm)
                                        if (qk_i - 2) in ss_done:
                                            W("scalar", s_pe, ss_done[qk_i - 2])
                                        P("scalar", lambda e, pslot=pslot, sqs=sqs: e.activation(
                                            out=sq_b[sqs][:], in_=ps_qkv[pslot][:], func=AF.Square,
                                            scale=1.0, bias=0.0).then_inc(s_act.h, 1))
                                        v_sq_box[0] = s_act.inc()
                                    return f
                                post0[g] = mk_post0_qk()

                                def mk_post1_qk(g=g, qk_i=qk_i, sqs=sqs, v_sq_box=v_sq_box):
                                    def f():
                                        W("tensor", s_act, v_sq_box[0])
                                        if (qk_i - 1) in sqrt_done:
                                            W("tensor", s_act, sqrt_done[qk_i - 1])
                                        P("tensor", lambda e, sqs=sqs: e.matmul(
                                            ps_ss[:], ones_cb[:], sq_b[sqs][:],
                                            start=True, stop=True).then_inc(s_pe.h, 1))
                                        v_ssv = s_pe.inc()
                                        ss_done[qk_i] = v_ssv
                                        W("scalar", s_pe, v_ssv)
                                        if (qk_i - 1) in recip_done:
                                            W("scalar", s_dve, recip_done[qk_i - 1])
                                        P("scalar", lambda e: e.activation(
                                            out=rn_r[:], in_=ps_ss[:], func=AF.Sqrt,
                                            scale=1.0, bias=eps6[:, 0:1]).then_inc(s_act.h, 1))
                                        v_rn0 = s_act.inc()
                                        sqrt_done[qk_i] = v_rn0
                                        W("vector", s_act, v_rn0)
                                        if (qk_i - 1) in bcmm_done:
                                            W("vector", s_pe, bcmm_done[qk_i - 1])
                                        P("vector", lambda e: e.reciprocal(rn2_r[:], rn_r[:]).then_inc(s_dve.h, 1))
                                        recip_done[qk_i] = s_dve.inc()
                                    return f
                                post1[g] = mk_post1_qk()

                                def mk_post2_qk(g=g, qk_i=qk_i, pslot=pslot, hs=hs,
                                                kind=kind, tt=tt, tslot=tslot, v_tab=v_tab):
                                    def f():
                                        bslot = qk_i % 2
                                        if bslot in bc_bank_cons:
                                            cs, cv = bc_bank_cons[bslot]
                                            W("tensor", cs, cv)
                                        W("tensor", s_dve, recip_done[qk_i])
                                        P("tensor", lambda e, bslot=bslot: e.matmul(
                                            ps_bc[bslot][:], ones_row[:], rn2_r[:],
                                            start=True, stop=True).then_inc(s_pe.h, 1))
                                        v_bc = s_pe.inc()
                                        bcmm_done[qk_i] = v_bc
                                        W("scalar", s_pe, v_bc)
                                        if bslot in bc_cons:
                                            cs, cv = bc_cons[bslot]
                                            W("scalar", cs, cv)
                                        P("scalar", lambda e, bslot=bslot: e.copy(
                                            out=bc_sb[bslot][:], in_=ps_bc[bslot][:]).then_inc(s_act.h, 1))
                                        v_bcp = s_act.inc()
                                        bc_bank_cons[bslot] = (s_act, v_bcp)
                                        W("vector", s_act, v_bcp)
                                        dst = QT if kind == 0 else KT
                                        if hs == 1:
                                            P("vector", lambda e, pslot=pslot, bslot=bslot, dst=dst, tt=tt: e.tensor_tensor(
                                                out=dst[:, S + tt * 512: S + tt * 512 + 512],
                                                in0=ps_qkv[pslot][:], in1=bc_sb[bslot][:],
                                                op=ALU.mult).then_inc(s_dve.h, 1))
                                            v_mul = s_dve.inc()
                                            grp_cons[g] = (s_dve, v_mul)
                                            bc_cons[bslot] = (s_dve, v_mul)
                                        else:
                                            W("vector", s_pe, swap_done[0])
                                            P("vector", lambda e, pslot=pslot, bslot=bslot: e.tensor_tensor(
                                                out=qn_f[:], in0=ps_qkv[pslot][:], in1=bc_sb[bslot][:],
                                                op=ALU.mult).then_inc(s_dve.h, 1))
                                            v_mul = s_dve.inc()
                                            grp_cons[g] = (s_dve, v_mul)
                                            W("tensor", s_dve, v_mul)
                                            P("tensor", lambda e, bslot=bslot: e.matmul(
                                                ps_bc[bslot][:], perm_a[:], qn_f[:],
                                                start=True, stop=True).then_inc(s_pe.h, 1))
                                            v_swap = s_pe.inc()
                                            swap_done[0] = v_swap
                                            W("vector", s_pe, v_swap)
                                            W("vector", s_act, v_tab)
                                            P("vector", lambda e, tslot=tslot: e.tensor_tensor(out=t64a[:], in0=qn_f[0:64, :], in1=cosT[tslot][:], op=ALU.mult))
                                            P("vector", lambda e, tslot=tslot, bslot=bslot: e.tensor_tensor(out=t64b[:], in0=ps_bc[bslot][0:64, :], in1=sinT[tslot][:], op=ALU.mult))
                                            P("vector", lambda e, tslot=tslot: e.tensor_tensor(out=t64c[:], in0=qn_f[0:64, :], in1=sinT[tslot][:], op=ALU.mult))
                                            P("vector", lambda e, tslot=tslot, bslot=bslot: e.tensor_tensor(out=t64d[:], in0=ps_bc[bslot][0:64, :], in1=cosT[tslot][:], op=ALU.mult).then_inc(s_dve.h, 1))
                                            W("vector", s_dve, s_dve.inc())
                                            P("vector", lambda e, dst=dst, tt=tt: e.tensor_tensor(
                                                out=dst[0:64, tt * 512: tt * 512 + 512],
                                                in0=t64a[:], in1=t64b[:], op=ALU.subtract))
                                            P("vector", lambda e, dst=dst, tt=tt: e.tensor_tensor(
                                                out=dst[64:128, tt * 512: tt * 512 + 512],
                                                in0=t64c[:], in1=t64d[:], op=ALU.add).then_inc(s_dve.h, 1))
                                            v_rope = s_dve.inc()
                                            bc_cons[bslot] = (s_dve, v_rope)
                                            bc_bank_cons[bslot] = (s_dve, v_rope)
                                    return f
                                post2[g] = mk_post2_qk()
                            else:
                                vs = g % 2
                                v_cp_box = [None]

                                def mk_post0_v(g=g, vs=vs, pslot=pslot, v_mm=v_mm, v_cp_box=v_cp_box):
                                    def f():
                                        W("scalar", s_pe, v_mm)
                                        if vs in vcopy_slot_cons:
                                            W("scalar", s_pe, vcopy_slot_cons[vs])
                                        P("scalar", lambda e, pslot=pslot, vs=vs: e.copy(
                                            out=v_sb[vs][:], in_=ps_qkv[pslot][:]).then_inc(s_act.h, 1))
                                        v_cp_box[0] = s_act.inc()
                                        grp_cons[g] = (s_act, v_cp_box[0])
                                    return f
                                post0[g] = mk_post0_v()

                                def mk_post1_v(g=g, vs=vs, v_cp_box=v_cp_box, hs=hs, tt=tt):
                                    def f():
                                        W("tensor", s_act, v_cp_box[0])
                                        if vtm_done[0] is not None:
                                            W("tensor", vtm_done[0][0], vtm_done[0][1])
                                        for b in range(4):
                                            if b < 3:
                                                P("tensor", lambda e, b=b, vs=vs: e.transpose(
                                                    ps_vt[:, b * 128:(b + 1) * 128],
                                                    v_sb[vs][:, b * 128:(b + 1) * 128], ident_b[:]))
                                            else:
                                                P("tensor", lambda e, b=b, vs=vs: e.transpose(
                                                    ps_vt[:, b * 128:(b + 1) * 128],
                                                    v_sb[vs][:, b * 128:(b + 1) * 128], ident_b[:]).then_inc(s_pe.h, 1))
                                        tr_done[g] = s_pe.inc()
                                        vcopy_slot_cons[vs] = tr_done[g]
                                    return f
                                post1[g] = mk_post1_v()

                                def mk_post2_v(g=g, hs=hs, tt=tt):
                                    def f():
                                        W("vector", s_pe, tr_done[g])
                                        P("vector", lambda e, hs=hs, tt=tt: e.tensor_copy(
                                            out=VT[:, hs * S + tt * 512: hs * S + tt * 512 + 512],
                                            in_=ps_vt[:]).then_inc(s_dve.h, 1))
                                        vtm_done[0] = (s_dve, s_dve.inc())
                                    return f
                                post2[g] = mk_post2_v()

                            gidx += 1

                        if tt == NT - 1:
                            G = gidx
                            post0.pop(G - 1)()
                            post2.pop(G - 3)()
                            post1.pop(G - 2)()
                            post2.pop(G - 2)()
                            post1.pop(G - 1)()
                            post2.pop(G - 1)()

                        pe_at_tt_end.append(s_pe.v)
                        dve_at_tt_end.append(s_dve.v)
                        if tt + 2 < NT:
                            load_xtile(tt + 2)

                plan.barrier(ALL_SEMS)

                with contextlib.ExitStack() as stC:
                    EC = stC.enter_context
                    ps_L = [EC(nc.psum_tensor(f"ps_L{i}", [128, 512], F32)) for i in range(2)]
                    ps_y = [EC(nc.psum_tensor(f"ps_y{i}", [128, 512], F32)) for i in range(2)]
                    ps_den = [EC(nc.psum_tensor(f"ps_den{i}", [1, 512], F32)) for i in range(2)]
                    ps_nb = EC(nc.psum_tensor("ps_nb", [128, 512], F32))
                    E_sb = [EC(nc.sbuf_tensor(f"E_sb{i}", [128, 512], BF16)) for i in range(3)]
                    rd_r = [EC(nc.sbuf_tensor(f"rd_r{q}", [1, 512], F32)) for q in range(2)]
                    nb_sb = [EC(nc.sbuf_tensor(f"nb_sb{q}", [128, 512], F32)) for q in range(2)]
                    yT_sb = [EC(nc.sbuf_tensor(f"yT_sb{q}", [128, 512], BF16)) for q in range(2)]

                    v_y_slot = [0, 0]
                    v_nbcopy = [0, 0]
                    v_rd_slot = [0, 0]
                    pending_tail = [None]

                    for hs in range(2):
                        for qc in range(NQ):
                            K_kt = 4 * qc + 4
                            sl = qc % 2
                            exp_done = {}
                            rdy = {}
                            cons = {}
                            log_done = {}

                            def mk_tail(hs=hs, qc=qc, sl=sl, cons=cons, K_kt=K_kt):
                                def f():
                                    W("vector", s_pe, cons[K_kt - 1])
                                    P("vector", lambda e, sl=sl: e.reciprocal(
                                        rd_r[sl][:], ps_den[sl][:]).then_inc(s_dve.h, 1))
                                    v_rd = s_dve.inc()
                                    v_rd_slot[sl] = v_rd
                                    W("tensor", s_dve, v_rd)
                                    W("tensor", s_act, v_nbcopy[1 - sl])
                                    P("tensor", lambda e, sl=sl: e.matmul(
                                        ps_nb[:], ones_row[:], rd_r[sl][:],
                                        start=True, stop=True).then_inc(s_pe.h, 1))
                                    v_nbm = s_pe.inc()
                                    W("scalar", s_pe, v_nbm)
                                    W("scalar", s_dve, v_y_slot[sl])
                                    P("scalar", lambda e, sl=sl: e.copy(
                                        out=nb_sb[sl][:], in_=ps_nb[:]).then_inc(s_act.h, 1))
                                    v_nbcopy[sl] = s_act.inc()
                                    W("vector", s_act, v_nbcopy[sl])
                                    W("vector", sy_st, sy_st.v)
                                    P("vector", lambda e, sl=sl: e.tensor_tensor(
                                        out=yT_sb[sl][:], in0=ps_y[sl][:], in1=nb_sb[sl][:],
                                        op=ALU.mult).then_inc(s_dve.h, 1))
                                    v_y = s_dve.inc()
                                    v_y_slot[sl] = v_y
                                    W("sync", s_dve, v_y)
                                    q0 = qc * 512
                                    off = 0
                                    while off < 512:
                                        cp = (q0 + off) // T
                                        seg = min(512 - off, T - ((q0 + off) % T))
                                        to = (q0 + off) % T
                                        P("sync", lambda e, cp=cp, hs=hs, to=to, off=off, seg=seg, sl=sl: e.dma_start(
                                            out=a2a_in[hs][cp, :, to:to + seg],
                                            in_=yT_sb[sl][:, off:off + seg]).then_inc(sy_st.h, 16))
                                        sy_st.inc(16)
                                        off += seg
                                return f

                            def emit_cons(kt, start, stop, hs=hs, sl=sl, cons=cons, rdy=rdy):
                                cs, cv = rdy[kt]
                                W("tensor", cs, cv)
                                if start:
                                    W("tensor", s_dve, v_y_slot[sl])
                                    W("tensor", s_dve, v_rd_slot[sl])
                                P("tensor", lambda e, kt=kt, sl=sl, start=start, stop=stop: e.matmul(
                                    ps_den[sl][:], ones_cb[:], E_sb[kt % 3][:],
                                    start=start, stop=stop))
                                P("tensor", lambda e, kt=kt, hs=hs, sl=sl, start=start, stop=stop: e.matmul(
                                    ps_y[sl][:], VT[:, hs * S + kt * 128: hs * S + kt * 128 + 128],
                                    E_sb[kt % 3][:], start=start, stop=stop).then_inc(s_pe.h, 1))
                                cons[kt] = s_pe.inc()

                            for kt in range(K_kt):
                                if kt >= 2:
                                    W("tensor", s_act, exp_done[kt - 2])
                                P("tensor", lambda e, kt=kt, hs=hs, qc=qc: e.matmul(
                                    ps_L[kt % 2][:],
                                    KT[:, hs * S + kt * 128: hs * S + kt * 128 + 128],
                                    QT[:, hs * S + qc * 512: hs * S + qc * 512 + 512],
                                    start=True, stop=True).then_inc(s_pe.h, 1))
                                log_done[kt] = s_pe.inc()
                                if kt >= 2:
                                    emit_cons(kt - 2, start=(kt == 2), stop=False)
                                if kt == 2 and pending_tail[0] is not None:
                                    pending_tail[0]()
                                    pending_tail[0] = None
                                W("scalar", s_pe, log_done[kt])
                                if kt >= 3:
                                    W("scalar", s_pe, cons.get(kt - 3, 0))
                                P("scalar", lambda e, kt=kt, hs=hs: e.activation(
                                    out=E_sb[kt % 3][:], in_=ps_L[kt % 2][:], func=AF.Exp,
                                    scale=gcol[:, hs:hs + 1], bias=gneg[:, hs:hs + 1]).then_inc(s_act.h, 1))
                                exp_done[kt] = s_act.inc()
                                j = kt - 4 * qc
                                if j >= 0:
                                    W("vector", s_act, exp_done[kt])
                                    P("vector", lambda e, kt=kt, j=j: e.tensor_tensor(
                                        out=E_sb[kt % 3][:], in0=E_sb[kt % 3][:],
                                        in1=masks[:, j * 512:(j + 1) * 512],
                                        op=ALU.mult).then_inc(s_dve.h, 1))
                                    rdy[kt] = (s_dve, s_dve.inc())
                                else:
                                    rdy[kt] = (s_act, exp_done[kt])
                            emit_cons(K_kt - 2, start=(K_kt == 2), stop=False)
                            emit_cons(K_kt - 1, start=False, stop=True)
                            pending_tail[0] = mk_tail()

                        pending_tail[0]()
                        pending_tail[0] = None
                        W("gpsimd", sy_st, sy_st.v)

                        def _do_a2a(e, hs=hs):
                            with allow_ici_all_to_all():
                                e.collective_compute(
                                    "AllToAll", ALU.bypass,
                                    replica_groups=[list(range(NCORES))],
                                    ins=[a2a_in[hs].ap().opt()],
                                    outs=[a2a_out[hs].ap().opt()]).then_inc(s_cc.h, 1)
                        P("gpsimd", _do_a2a)
                        s_cc.inc()

            x1 = E(nc.sbuf_tensor("x1", [128, DC * T], F32))
            x1_bf = E(nc.sbuf_tensor("x1_bf", [128, DC * T], BF16))
            norm_done = [0]

            def emit_layernorm(xbuf_t, out_bf, tag, store_cb=None):
                with contextlib.ExitStack() as stX:
                    EX = stX.enter_context
                    ps_sum = [EX(nc.psum_tensor(f"lps_s{i}{tag}", [1, 512], F32)) for i in range(len(tspans))]
                    ps_sq = [EX(nc.psum_tensor(f"lps_q{i}{tag}", [1, 512], F32)) for i in range(len(tspans))]
                    ps_bm = EX(nc.psum_tensor(f"lps_bm{tag}", [128, 512], F32))
                    ps_br = EX(nc.psum_tensor(f"lps_br{tag}", [128, 512], F32))
                    sqb = [EX(nc.sbuf_tensor(f"ln_sqb{i}{tag}", [128, 512], BF16)) for i in range(2)]
                    mean_r = EX(nc.sbuf_tensor(f"ln_mean{tag}", [1, T], F32))
                    ex2_r = EX(nc.sbuf_tensor(f"ln_ex2{tag}", [1, T], F32))
                    rstd_r = EX(nc.sbuf_tensor(f"ln_rstd{tag}", [1, T], F32))
                    msq_r = EX(nc.sbuf_tensor(f"ln_msq{tag}", [1, T], F32))
                    tmp2_f = [EX(nc.sbuf_tensor(f"ln_tmp{i}{tag}", [128, 512], F32)) for i in range(2)]

                    v_acc = {}
                    v_sqmm = {}
                    for ti, (t0, tl) in enumerate(tspans):
                        for dc in range(DC):
                            gi = ti * DC + dc
                            if gi >= 2:
                                W("scalar", s_pe, v_sqmm[gi - 2])
                            P("scalar", lambda e, dc=dc, t0=t0, tl=tl, gi=gi: e.activation(
                                out=sqb[gi % 2][:, 0:tl], in_=xbuf_t[:, dc * T + t0: dc * T + t0 + tl],
                                func=AF.Square, scale=1.0, bias=0.0).then_inc(s_act.h, 1))
                            v_sq = s_act.inc()
                            P("tensor", lambda e, dc=dc, ti=ti, t0=t0, tl=tl: e.matmul(
                                ps_sum[ti][:, 0:tl], ones_cf[:],
                                xbuf_t[:, dc * T + t0: dc * T + t0 + tl],
                                start=(dc == 0), stop=(dc == DC - 1)))
                            W("tensor", s_act, v_sq)
                            P("tensor", lambda e, dc=dc, ti=ti, tl=tl, gi=gi: e.matmul(
                                ps_sq[ti][:, 0:tl], ones_cb[:], sqb[gi % 2][:, 0:tl],
                                start=(dc == 0), stop=(dc == DC - 1)).then_inc(s_pe.h, 1))
                            v_sqmm[gi] = s_pe.inc()
                        v_acc[ti] = v_sqmm[ti * DC + DC - 1]

                    v_bc_prev = 0
                    for ti, (t0, tl) in enumerate(tspans):
                        W("scalar", s_pe, v_acc[ti])
                        P("scalar", lambda e, ti=ti, t0=t0, tl=tl: e.activation(
                            out=mean_r[:, t0:t0 + tl], in_=ps_sum[ti][:, 0:tl],
                            func=AF.Identity, scale=1.0 / DM, bias=0.0))
                        P("scalar", lambda e, ti=ti, t0=t0, tl=tl: e.activation(
                            out=ex2_r[:, t0:t0 + tl], in_=ps_sq[ti][:, 0:tl],
                            func=AF.Identity, scale=1.0 / DM, bias=0.0).then_inc(s_act.h, 1))
                        v_st = s_act.inc()
                        W("vector", s_act, v_st)
                        P("vector", lambda e, t0=t0, tl=tl: e.tensor_tensor(
                            out=msq_r[:, t0:t0 + tl], in0=mean_r[:, t0:t0 + tl],
                            in1=mean_r[:, t0:t0 + tl], op=ALU.mult).then_inc(s_dve.h, 1))
                        v_msq = s_dve.inc()
                        W("vector", s_dve, v_msq)
                        P("vector", lambda e, t0=t0, tl=tl: e.tensor_tensor(
                            out=ex2_r[:, t0:t0 + tl], in0=ex2_r[:, t0:t0 + tl],
                            in1=msq_r[:, t0:t0 + tl], op=ALU.subtract).then_inc(s_dve.h, 1))
                        v_var = s_dve.inc()
                        W("scalar", s_dve, v_var)
                        P("scalar", lambda e, t0=t0, tl=tl: e.activation(
                            out=msq_r[:, t0:t0 + tl], in_=ex2_r[:, t0:t0 + tl],
                            func=AF.Sqrt, scale=1.0, bias=eps5[:, 0:1]).then_inc(s_act.h, 1))
                        v_sd = s_act.inc()
                        W("vector", s_act, v_sd)
                        P("vector", lambda e, t0=t0, tl=tl: e.reciprocal(
                            rstd_r[:, t0:t0 + tl], msq_r[:, t0:t0 + tl]).then_inc(s_dve.h, 1))
                        v_rstd = s_dve.inc()
                        W("tensor", s_dve, v_rstd)
                        if ti >= 1:
                            W("tensor", s_dve, norm_done[0])
                        P("tensor", lambda e, t0=t0, tl=tl: e.matmul(
                            ps_bm[:, 0:tl], ones_row[:], mean_r[:, t0:t0 + tl],
                            start=True, stop=True))
                        P("tensor", lambda e, t0=t0, tl=tl: e.matmul(
                            ps_br[:, 0:tl], ones_row[:], rstd_r[:, t0:t0 + tl],
                            start=True, stop=True).then_inc(s_pe.h, 1))
                        v_bc = s_pe.inc()
                        W("vector", s_pe, v_bc)
                        v_sub_at = {}
                        v_mult_at = {}

                        def _norm_mult(dc, t0=t0, tl=tl, ti=ti):
                            W("vector", s_dve, v_sub_at[dc])
                            P("vector", lambda e, dc=dc: e.tensor_tensor(
                                out=xbuf_t[:, dc * T + t0: dc * T + t0 + tl],
                                in0=tmp2_f[dc % 2][:, 0:tl], in1=ps_br[:, 0:tl],
                                op=ALU.mult).then_inc(s_dve.h, 1))
                            v_n = s_dve.inc()
                            v_mult_at[dc] = v_n
                            if store_cb is not None and ti == len(tspans) - 1:
                                store_cb(dc, v_n)
                            if out_bf is not None:
                                W("scalar", s_dve, v_n)
                                P("scalar", lambda e, dc=dc: e.copy(
                                    out=out_bf[:, dc * T + t0: dc * T + t0 + tl],
                                    in_=xbuf_t[:, dc * T + t0: dc * T + t0 + tl]).then_inc(s_act.h, 1))
                                s_act.inc()

                        for dc in range(DC):
                            if dc >= 2:
                                W("vector", s_dve, v_mult_at[dc - 2])
                            P("vector", lambda e, dc=dc, t0=t0, tl=tl: e.tensor_tensor(
                                out=tmp2_f[dc % 2][:, 0:tl], in0=xbuf_t[:, dc * T + t0: dc * T + t0 + tl],
                                in1=ps_bm[:, 0:tl], op=ALU.subtract).then_inc(s_dve.h, 1))
                            v_sub_at[dc] = s_dve.inc()
                            if dc >= 1:
                                _norm_mult(dc - 1)
                        _norm_mult(DC - 1)
                        norm_done[0] = s_dve.v

            with contextlib.ExitStack() as stE:
                EE = stE.enter_context
                ybuf = EE(nc.sbuf_tensor("ybuf", [128, DC * T], BF16))
                wout_sb = [EE(nc.sbuf_tensor(f"wout_sb{i}", [128, DC * 512], BF16)) for i in range(2)]
                xres_sb = [EE(nc.sbuf_tensor(f"xres_sb{i}", [128, T], F32)) for i in range(2)]
                ps_o = [EE(nc.psum_tensor(f"ps_o{i}", [128, 512], F32)) for i in range(2)]

                W("sync", s_cc, 1)
                P("sync", lambda e: e.dma_start(
                    out=bass.AP(ybuf, 0, [[DC * T, 128], [2 * T, DC // 2], [1, T]]),
                    in_=bass.AP(a2a_out[0], 0, [[T, 128], [128 * T, DC // 2], [1, T]])).then_inc(sy_ld.h, 16))
                sy_ld.inc(16)
                W("sync", s_cc, 2)
                P("sync", lambda e: e.dma_start(
                    out=bass.AP(ybuf, T, [[DC * T, 128], [2 * T, DC // 2], [1, T]]),
                    in_=bass.AP(a2a_out[1], 0, [[T, 128], [128 * T, DC // 2], [1, T]])).then_inc(sy_ld.h, 16))
                v_yb = sy_ld.inc(16)

                xres_n = [0, 0]
                xres_cons = {}
                evac_lin = {}

                def load_xres(fc):
                    slot = fc % 2
                    if fc >= 2:
                        W("sync", s_dve, xres_cons[fc - 2])
                    P("sync", lambda e, fc=fc, slot=slot: e.dma_start(
                        out=xres_sb[slot][:], in_=xres[:, fc * T:(fc + 1) * T]).then_inc(sy_r[slot].h, 16))
                    xres_n[slot] += 1
                    sy_r[slot].inc(16)

                load_xres(0)
                if DC > 1:
                    load_xres(1)

                v_wo_half = {}
                mm_done_q = {}

                def load_wo(fh):
                    if fh >= 2:
                        W("gpsimd", s_pe, mm_done_q[fh - 2])
                    P("gpsimd", lambda e, fh=fh: e.dma_start(
                        out=wout_sb[fh % 2][:], in_=wo[fh]).then_inc(sq_wo[fh % 2].h, 16))
                    v_wo_half[fh] = sq_wo[fh % 2].inc(16)

                load_wo(0)
                load_wo(1)
                for fh in range(4):
                    for fc_l in range(4):
                        fc = fh * 4 + fc_l
                        ntsp = len(tspans)
                        for ti, (t0, tl) in enumerate(tspans):
                            li = fc * ntsp + ti
                            pslot = li % 2
                            W("tensor", sy_ld, v_yb)
                            W("tensor", sq_wo[fh % 2], v_wo_half[fh])
                            if li >= 2:
                                W("tensor", s_dve, evac_lin[li - 2])
                            for r in range(DC):
                                P("tensor", lambda e, fh=fh, fc_l=fc_l, r=r, t0=t0, tl=tl, pslot=pslot: (
                                    e.matmul(
                                        ps_o[pslot][:, 0:tl],
                                        wout_sb[fh % 2][:, r * 512 + fc_l * 128: r * 512 + fc_l * 128 + 128],
                                        ybuf[:, r * T + t0: r * T + t0 + tl],
                                        start=(r == 0), stop=(r == DC - 1)
                                    ).then_inc(s_pe.h, 1) if r == DC - 1 else e.matmul(
                                        ps_o[pslot][:, 0:tl],
                                        wout_sb[fh % 2][:, r * 512 + fc_l * 128: r * 512 + fc_l * 128 + 128],
                                        ybuf[:, r * T + t0: r * T + t0 + tl],
                                        start=(r == 0), stop=(r == DC - 1))))
                            v_mm = s_pe.inc()
                            W("vector", s_pe, v_mm)
                            W("vector", sy_r[fc % 2], 16 * xres_n[fc % 2])
                            P("vector", lambda e, fc=fc, t0=t0, tl=tl, pslot=pslot: e.tensor_tensor(
                                out=x1[:, fc * T + t0: fc * T + t0 + tl],
                                in0=ps_o[pslot][:, 0:tl], in1=xres_sb[fc % 2][:, t0:t0 + tl],
                                op=ALU.add).then_inc(s_dve.h, 1))
                            evac_lin[li] = s_dve.inc()
                        xres_cons[fc] = s_dve.v
                        if fc + 2 < DC:
                            load_xres(fc + 2)
                    mm_done_q[fh] = s_pe.v
                    if fh + 2 < 4:
                        load_wo(fh + 2)

                plan.barrier(ALL_SEMS)

            emit_layernorm(x1, x1_bf, "a")
            plan.barrier(ALL_SEMS)

            comb_all = E(nc.sbuf_tensor("comb_all", [128, NT8 * NEXP], F32))
            with contextlib.ExitStack() as stG:
                EG = stG.enter_context
                ps_s8 = [EG(nc.psum_tensor(f"ps_s8{i}", [128, NEXP], F32)) for i in range(2)]
                sc_f = EG(nc.sbuf_tensor("sc_f", [128, NEXP], F32))
                sc_t = EG(nc.sbuf_tensor("sc_t", [128, NEXP], F32))
                sc_m = EG(nc.sbuf_tensor("sc_m", [128, 1], F32))
                sc_m2 = EG(nc.sbuf_tensor("sc_m2", [128, 1], F32))

                s8_cons = {}
                for tt8 in range(NT8):
                    s8s = tt8 % 2
                    if tt8 >= 2:
                        W("tensor", s_dve, s8_cons[tt8 - 2])
                    for dc in range(DC):
                        P("tensor", lambda e, dc=dc, tt8=tt8, s8s=s8s: (
                            e.matmul(
                                ps_s8[s8s][:],
                                x1_bf[:, dc * T + tt8 * 128: dc * T + tt8 * 128 + 128],
                                gw_sb[:, dc * NEXP:(dc + 1) * NEXP],
                                start=(dc == 0), stop=(dc == DC - 1)
                            ).then_inc(s_pe.h, 1) if dc == DC - 1 else e.matmul(
                                ps_s8[s8s][:],
                                x1_bf[:, dc * T + tt8 * 128: dc * T + tt8 * 128 + 128],
                                gw_sb[:, dc * NEXP:(dc + 1) * NEXP],
                                start=(dc == 0), stop=(dc == DC - 1))))
                    v_g8 = s_pe.inc()
                    W("vector", s_pe, v_g8)

                    def CH(fn):
                        W("vector", s_dve, s_dve.v)
                        P("vector", lambda e, fn=fn: fn(e).then_inc(s_dve.h, 1))
                        s_dve.inc()

                    CH(lambda e, s8s=s8s: e.reduce_max(sc_m[:], ps_s8[s8s][:], axis=AX.X))
                    CH(lambda e, s8s=s8s: e.tensor_single_scalar(
                        out=sc_f[:], in_=ps_s8[s8s][:], scalar=sc_m[:, 0:1], op=ALU.subtract))
                    v_sub = s_dve.v
                    s8_cons[tt8] = v_sub
                    W("scalar", s_dve, v_sub)
                    P("scalar", lambda e: e.activation(out=sc_f[:], in_=sc_f[:], func=AF.Exp,
                                                       scale=1.0, bias=0.0).then_inc(s_act.h, 1))
                    v_exp = s_act.inc()
                    W("vector", s_act, v_exp)
                    CH(lambda e: e.reduce_sum(sc_m[:], sc_f[:], axis=AX.X))
                    CH(lambda e: e.reciprocal(sc_m[:], sc_m[:]))
                    CH(lambda e: e.tensor_single_scalar(
                        out=sc_f[:], in_=sc_f[:], scalar=sc_m[:, 0:1], op=ALU.mult))
                    CH(lambda e: e.reduce_max(sc_m[:], sc_f[:], axis=AX.X))
                    CH(lambda e: e.tensor_single_scalar(
                        out=sc_t[:], in_=sc_f[:], scalar=sc_m[:, 0:1], op=ALU.is_equal))
                    CH(lambda e: e.scalar_tensor_tensor(
                        out=sc_t[:], in0=sc_t[:], scalar=-2.0, in1=sc_f[:],
                        op0=ALU.mult, op1=ALU.add))
                    CH(lambda e: e.reduce_max(sc_m2[:], sc_t[:], axis=AX.X))
                    CH(lambda e: e.tensor_single_scalar(
                        out=sc_t[:], in_=sc_f[:], scalar=sc_m2[:, 0:1], op=ALU.is_ge))
                    CH(lambda e, tt8=tt8: e.tensor_tensor(
                        out=comb_all[:, tt8 * NEXP:(tt8 + 1) * NEXP],
                        in0=sc_f[:], in1=sc_t[:], op=ALU.mult))

                plan.barrier(ALL_SEMS)

            with contextlib.ExitStack() as stF:
                EF = stF.enter_context
                G_sb = EF(nc.sbuf_tensor("G_sb", [128, HE * T], BF16))
                wgu_sb = [EF(nc.sbuf_tensor(f"wgu_sb{i}", [128, DC * 128], BF16)) for i in range(4)]
                wd_sb = EF(nc.sbuf_tensor("wd_sb", [128, HE * DM], BF16))
                combT = EF(nc.sbuf_tensor("combT", [1, T], F32))
                comb_bc = EF(nc.sbuf_tensor("comb_bc", [128, T], F32))
                sg_sb = [EF(nc.sbuf_tensor(f"sg_sb{i}", [128, 512], BF16)) for i in range(2)]
                u2_f = [EF(nc.sbuf_tensor(f"u2_f{i}", [128, 512], F32)) for i in range(2)]
                ps_gu = [EF(nc.psum_tensor(f"ps_gu{i}", [128, 512], F32)) for i in range(4)]
                ps_dn = [EF(nc.psum_tensor(f"ps_dn{i}", [128, 512], F32)) for i in range(2)]

                passes = [(0, ei) for ei in range(NEXP)] + [(1, 0), (1, 1)]
                nchunks = len(passes) * HE
                chunk_cons = {}
                load_n = [0, 0, 0, 0]
                load_ord = [0]
                dn_mm_done = {}

                def chunk_src(ci):
                    pi, h = divmod(ci, HE)
                    kind, ei = passes[pi]
                    if kind == 0:
                        return wgate[ei, h], wup[ei, h]
                    return swg[ei * HE + h], swu[ei * HE + h]

                def load_gu():
                    ci = load_ord[0]
                    load_ord[0] += 1
                    gslot = ci % 2
                    if ci >= 2 and (ci - 2) in chunk_cons:
                        cs, cv = chunk_cons[ci - 2]
                        W("gpsimd", cs, cv)
                    sg_, su_ = chunk_src(ci)
                    P("gpsimd", lambda e, s=sg_, gslot=gslot: e.dma_start(
                        out=wgu_sb[2 * gslot][:], in_=s).then_inc(sq_g[2 * gslot].h, 16))
                    load_n[2 * gslot] += 1
                    sq_g[2 * gslot].inc(16)
                    P("gpsimd", lambda e, s=su_, gslot=gslot: e.dma_start(
                        out=wgu_sb[2 * gslot + 1][:], in_=s).then_inc(sq_g[2 * gslot + 1].h, 16))
                    load_n[2 * gslot + 1] += 1
                    sq_g[2 * gslot + 1].inc(16)

                def load_wd(pi):
                    kind, ei = passes[pi]
                    if pi >= 1:
                        W("gpsimd", s_pe, dn_mm_done[pi - 1])
                    for h in range(HE):
                        src = wdown[ei, h] if kind == 0 else swd[ei * HE + h]
                        P("gpsimd", lambda e, src=src, h=h: e.dma_start(
                            out=wd_sb[:, h * DM:(h + 1) * DM], in_=src).then_inc(sq_d.h, 16))
                        sq_d.inc(16)

                load_gu()
                load_gu()
                load_wd(0)

                last_gu_cons = {}
                prev_pass_G_done = 0
                prev_pass_dn_done = 0
                for pi, (kind, ei) in enumerate(passes):
                    v_combbc = 0
                    if kind == 0:
                        v_ctcp_prev = 0
                        for ti in range(len(tspans)):
                            if ti in last_gu_cons:
                                cs, cv = last_gu_cons[ti]
                                W("tensor", cs, cv)
                        for tt8 in range(NT8):
                            if tt8 >= 1:
                                W("tensor", s_act, v_ctcp_prev)
                            P("tensor", lambda e, tt8=tt8, ei=ei: e.transpose(
                                ps_gu[1][0:1, 0:128], comb_all[:, tt8 * NEXP + ei: tt8 * NEXP + ei + 1],
                                ident_f[:]).then_inc(s_pe.h, 1))
                            v_tr = s_pe.inc()
                            W("scalar", s_pe, v_tr)
                            P("scalar", lambda e, tt8=tt8: e.copy(
                                out=combT[:, tt8 * 128:(tt8 + 1) * 128],
                                in_=ps_gu[1][0:1, 0:128]).then_inc(s_act.h, 1))
                            v_ctcp_prev = s_act.inc()
                        for ti, (t0, tl) in enumerate(tspans):
                            W("tensor", s_act, v_ctcp_prev)
                            P("tensor", lambda e, t0=t0, tl=tl, ti=ti: e.matmul(
                                ps_gu[ti][:, 0:tl], ones_row[:], combT[:, t0:t0 + tl],
                                start=True, stop=True).then_inc(s_pe.h, 1))
                            v_cbm = s_pe.inc()
                            W("scalar", s_pe, v_cbm)
                            P("scalar", lambda e, t0=t0, tl=tl, ti=ti: e.copy(
                                out=comb_bc[:, t0:t0 + tl], in_=ps_gu[ti][:, 0:tl]).then_inc(s_act.h, 1))
                            v_combbc = s_act.inc()
                            last_gu_cons[ti] = (s_act, v_combbc)

                    for h in range(HE):
                        ci = pi * HE + h
                        gslot = ci % 2
                        if pi >= 1 and h == 0:
                            W("vector", s_pe, prev_pass_dn_done)
                        for ti, (t0, tl) in enumerate(tspans):
                            pg = ps_gu[ti]
                            pu = ps_gu[2 + ti]
                            W("tensor", sq_g[2 * gslot], 16 * load_n[2 * gslot])
                            W("tensor", sq_g[2 * gslot + 1], 16 * load_n[2 * gslot + 1])
                            if ti in last_gu_cons:
                                cs, cv = last_gu_cons[ti]
                                W("tensor", cs, cv)
                            for dc in range(DC):
                                P("tensor", lambda e, gslot=gslot, dc=dc, t0=t0, tl=tl, pg=pg: (
                                    e.matmul(
                                        pg[:, 0:tl], wgu_sb[2 * gslot][:, dc * 128:(dc + 1) * 128],
                                        x1_bf[:, dc * T + t0: dc * T + t0 + tl],
                                        start=(dc == 0), stop=(dc == DC - 1)
                                    ).then_inc(s_pe.h, 1) if dc == DC - 1 else e.matmul(
                                        pg[:, 0:tl], wgu_sb[2 * gslot][:, dc * 128:(dc + 1) * 128],
                                        x1_bf[:, dc * T + t0: dc * T + t0 + tl],
                                        start=(dc == 0), stop=(dc == DC - 1))))
                            v_gd = s_pe.inc()
                            for dc in range(DC):
                                P("tensor", lambda e, gslot=gslot, dc=dc, t0=t0, tl=tl, pu=pu: (
                                    e.matmul(
                                        pu[:, 0:tl], wgu_sb[2 * gslot + 1][:, dc * 128:(dc + 1) * 128],
                                        x1_bf[:, dc * T + t0: dc * T + t0 + tl],
                                        start=(dc == 0), stop=(dc == DC - 1)
                                    ).then_inc(s_pe.h, 1) if dc == DC - 1 else e.matmul(
                                        pu[:, 0:tl], wgu_sb[2 * gslot + 1][:, dc * 128:(dc + 1) * 128],
                                        x1_bf[:, dc * T + t0: dc * T + t0 + tl],
                                        start=(dc == 0), stop=(dc == DC - 1))))
                            v_ud = s_pe.inc()
                            W("scalar", s_pe, v_gd)
                            P("scalar", lambda e, tl=tl, pg=pg, ti=ti: e.activation(
                                out=sg_sb[ti][:, 0:tl], in_=pg[:, 0:tl], func=AF.Silu,
                                scale=1.0, bias=0.0).then_inc(s_act.h, 1))
                            v_sg = s_act.inc()
                            W("vector", s_pe, v_ud)
                            W("vector", s_act, v_sg)
                            if kind == 0:
                                P("vector", lambda e, t0=t0, tl=tl, pu=pu, ti=ti: e.tensor_tensor(
                                    out=u2_f[ti][:, 0:tl], in0=pu[:, 0:tl],
                                    in1=comb_bc[:, t0:t0 + tl], op=ALU.mult).then_inc(s_dve.h, 1))
                                v_u2 = s_dve.inc()
                                W("vector", s_dve, v_u2)
                                P("vector", lambda e, h=h, t0=t0, tl=tl, ti=ti: e.tensor_tensor(
                                    out=G_sb[:, h * T + t0: h * T + t0 + tl],
                                    in0=u2_f[ti][:, 0:tl], in1=sg_sb[ti][:, 0:tl],
                                    op=ALU.mult).then_inc(s_dve.h, 1))
                            else:
                                P("vector", lambda e, h=h, t0=t0, tl=tl, pu=pu, ti=ti: e.tensor_tensor(
                                    out=G_sb[:, h * T + t0: h * T + t0 + tl],
                                    in0=pu[:, 0:tl], in1=sg_sb[ti][:, 0:tl],
                                    op=ALU.mult).then_inc(s_dve.h, 1))
                            vv = s_dve.inc()
                            last_gu_cons[ti] = (s_dve, vv)
                            chunk_cons[ci] = (s_dve, vv)
                        if load_ord[0] < nchunks:
                            load_gu()
                    v_G_all = s_dve.v

                    W("tensor", sq_d, sq_d.v)
                    W("tensor", s_dve, v_G_all)
                    dn_evac = {}
                    ntsp = len(tspans)
                    for fc in range(DC):
                        for ti, (t0, tl) in enumerate(tspans):
                            di = fc * ntsp + ti
                            dslot = di % 2
                            if di >= 2:
                                W("tensor", s_dve, dn_evac[di - 2])
                            for h in range(HE):
                                P("tensor", lambda e, h=h, fc=fc, t0=t0, tl=tl, dslot=dslot: (
                                    e.matmul(
                                        ps_dn[dslot][:, 0:tl],
                                        wd_sb[:, h * DM + fc * 128: h * DM + fc * 128 + 128],
                                        G_sb[:, h * T + t0: h * T + t0 + tl],
                                        start=(h == 0), stop=(h == HE - 1)
                                    ).then_inc(s_pe.h, 1) if h == HE - 1 else e.matmul(
                                        ps_dn[dslot][:, 0:tl],
                                        wd_sb[:, h * DM + fc * 128: h * DM + fc * 128 + 128],
                                        G_sb[:, h * T + t0: h * T + t0 + tl],
                                        start=(h == 0), stop=(h == HE - 1))))
                            v_dn = s_pe.inc()
                            W("vector", s_pe, v_dn)
                            P("vector", lambda e, fc=fc, t0=t0, tl=tl, dslot=dslot: e.tensor_tensor(
                                out=x1[:, fc * T + t0: fc * T + t0 + tl],
                                in0=ps_dn[dslot][:, 0:tl],
                                in1=x1[:, fc * T + t0: fc * T + t0 + tl],
                                op=ALU.add).then_inc(s_dve.h, 1))
                            dn_evac[di] = s_dve.inc()
                    dn_mm_done[pi] = s_pe.v
                    prev_pass_dn_done = s_pe.v
                    if pi + 1 < len(passes):
                        load_wd(pi + 1)

                plan.barrier(ALL_SEMS)

            def _store_chunk(dc, v_n):
                W("sync", s_dve, v_n)
                P("sync", lambda e, dc=dc: e.dma_start(
                    out=out_d[:, dc * T:(dc + 1) * T],
                    in_=x1[:, dc * T:(dc + 1) * T]).then_inc(sy_st.h, 16))
                sy_st.inc(16)

            emit_layernorm(x1, None, "b", store_cb=_store_chunk)
            plan.barrier(ALL_SEMS)
            fin = sy_st.v
            P("sync", lambda e, fin=fin: e.wait_ge(sy_st.h, fin))

        with nc.Block() as blk:
            @blk.tensor
            def _(eng):
                for f in plan.ops["tensor"]:
                    f(eng)

            @blk.vector
            def _(eng):
                for f in plan.ops["vector"]:
                    f(eng)

            @blk.scalar
            def _(eng):
                for f in plan.ops["scalar"]:
                    f(eng)

            @blk.gpsimd
            def _(eng):
                for f in plan.ops["gpsimd"]:
                    f(eng)

            @blk.sync
            def _(eng):
                for f in plan.ops["sync"]:
                    f(eng)

    return nc



def _prep_inputs(S, L0_attn_x, L0_attn_w_qkv, L0_attn_w_out, L0_attn_q_gain,
                 L0_ffn_gate_w, L0_ffn_w_gate, L0_ffn_w_up, L0_ffn_w_down,
                 L0_ffn_sw_gate, L0_ffn_sw_up, L0_ffn_sw_down):
    T = S // NCORES
    NT = S // 512
    f32 = np.float32
    C = np.ascontiguousarray
    x2d = np.asarray(L0_attn_x, dtype=f32).reshape(S, DM)
    wqkv = np.asarray(L0_attn_w_qkv, dtype=f32)
    w_out = np.asarray(L0_attn_w_out, dtype=f32)
    q_gain = np.asarray(L0_attn_q_gain, dtype=f32)
    gate_w = np.asarray(L0_ffn_gate_w, dtype=f32)
    w_gate = np.asarray(L0_ffn_w_gate, dtype=f32)
    w_up = np.asarray(L0_ffn_w_up, dtype=f32)
    w_down = np.asarray(L0_ffn_w_down, dtype=f32)
    sw_gate = np.asarray(L0_ffn_sw_gate, dtype=f32)
    sw_up = np.asarray(L0_ffn_sw_up, dtype=f32)
    sw_down = np.asarray(L0_ffn_sw_down, dtype=f32)

    xt = C(x2d.reshape(NT, 512, DC, 128).transpose(0, 3, 2, 1).reshape(NT, 128, DC * 512))
    wo_t = C(w_out.reshape(DC, 128, 4, 512).transpose(2, 1, 0, 3).reshape(4, 128, DC * 512))
    gw_t = C(gate_w.reshape(DC, 128, NEXP).transpose(1, 0, 2).reshape(128, DC * NEXP))
    wg_t = C(w_gate.reshape(NEXP, DC, 128, HE, 128).transpose(0, 3, 2, 1, 4).reshape(NEXP, HE, 128, DC * 128))
    wu_t = C(w_up.reshape(NEXP, DC, 128, HE, 128).transpose(0, 3, 2, 1, 4).reshape(NEXP, HE, 128, DC * 128))
    wd_t = C(w_down.reshape(NEXP, HE, 128, DM))
    swg_t = C(sw_gate.reshape(DC, 128, HS2, 128).transpose(2, 1, 0, 3).reshape(HS2, 128, DC * 128))
    swu_t = C(sw_up.reshape(DC, 128, HS2, 128).transpose(2, 1, 0, 3).reshape(HS2, 128, DC * 128))
    swd_t = C(sw_down.reshape(HS2, 128, DM))

    in_maps = []
    for c in range(NCORES):
        def wslice(base):
            cols = slice(base + 2 * c * 128, base + 2 * c * 128 + 256)
            return C(wqkv[:, cols].reshape(DC, 128, 256).transpose(1, 0, 2).reshape(128, DC * 256))

        xres_c = C(x2d[c * T:(c + 1) * T].reshape(T, DC, 128).transpose(2, 1, 0).reshape(128, DC * T))
        in_maps.append({
            "xt": xt,
            "xres": xres_c,
            "wq": wslice(0),
            "wk": wslice(DM),
            "wv": wslice(2 * DM),
            "wo": wo_t,
            "qg": C(q_gain[2 * c:2 * c + 2].reshape(1, 2)),
            "gw": gw_t,
            "wgate": wg_t,
            "wup": wu_t,
            "wdown": wd_t,
            "swg": swg_t,
            "swu": swu_t,
            "swd": swd_t,
        })
    return in_maps


def _assemble(S, results):
    T = S // NCORES
    full = np.empty((1, S, DM), dtype=np.float32)
    for c in range(NCORES):
        arr = np.asarray(results[c]["out"]).reshape(128, DC, T)
        full[0, c * T:(c + 1) * T, :] = arr.transpose(2, 1, 0).reshape(T, DM)
    return full


def kernel(L0_attn_x, L0_attn_w_qkv, L0_attn_w_out, L0_attn_q_gain,
           L0_ffn_gate_w, L0_ffn_w_gate, L0_ffn_w_up, L0_ffn_w_down,
           L0_ffn_sw_gate, L0_ffn_sw_up, L0_ffn_sw_down):
    S = int(np.asarray(L0_attn_x).shape[1])
    in_maps = _prep_inputs(S, L0_attn_x, L0_attn_w_qkv, L0_attn_w_out,
                           L0_attn_q_gain, L0_ffn_gate_w, L0_ffn_w_gate,
                           L0_ffn_w_up, L0_ffn_w_down, L0_ffn_sw_gate,
                           L0_ffn_sw_up, L0_ffn_sw_down)
    nc = build_nc(S)
    res = run_bass_kernel_spmd(nc, in_maps, core_ids=list(range(NCORES)))
    return _assemble(S, res.results)
